# version 17
# baseline (speedup 1.0000x reference)
import numpy as np
import ml_dtypes
import concourse.bass as bass
import concourse.mybir as mybir
from concourse.bass_utils import run_bass_kernel_spmd

F32 = mybir.dt.float32
BF16 = mybir.dt.bfloat16
U8 = mybir.dt.uint8
AF = mybir.ActivationFunctionType
ALU = mybir.AluOpType
AX = mybir.AxisListType

D = 1024
NSUB = 4
TT = 128 * NSUB
NE = 8
NH = 4
NCORE = 8


class Plan:

    ENGS = ("tensor", "vector", "scalar", "sync", "gpsimd")

    def __init__(self):
        self.ops = {e: [] for e in self.ENGS}
        self.cnt = {s: 0 for s in ("sT", "sV", "sS", "sQc", "sQx0", "sQx1",
                                    "sGs", "sGw0", "sGw1", "sGw2", "sO0", "sO1",
                                    "sB0", "sB1")}

    def emit(self, eng, fn, waits=(), inc=None, inc_by=1):
        for s, v in waits:
            assert v <= self.cnt[s], f"wait {s}>={v} but only {self.cnt[s]} emitted"
        if inc is not None:
            self.cnt[inc] += inc_by
        self.ops[eng].append((fn, tuple(waits), inc, inc_by))
        return self.cnt[inc] if inc is not None else None

    def replay(self, eng_name, eng, sems):
        last = {}
        for fn, waits, inc, inc_by in self.ops[eng_name]:
            for s, v in waits:
                if last.get(s, -1) < v:
                    eng.wait_ge(sems[s], v)
                    last[s] = v
            inst = fn(eng)
            if inc is not None:
                inst.then_inc(sems[inc], inc_by)


def build_program(n_tiles, dbg=None):
    nc = bass.Bass("TRN2", target_bir_lowering=False, detect_race_conditions=False)
    ntok = TT * n_tiles

    x_h = nc.dram_tensor("x", [ntok, D], F32, kind="ExternalInput").ap()
    wfcT_h = nc.dram_tensor("wfcT", [D, D], F32, kind="ExternalInput").ap()
    wprojT_h = nc.dram_tensor("wprojT", [D, D], F32, kind="ExternalInput").ap()
    gwT_h = nc.dram_tensor("gwT", [D, NE], F32, kind="ExternalInput").ap()
    ew_h = nc.dram_tensor("ew", [NE, D, D], F32, kind="ExternalInput").ap()
    eb_h = nc.dram_tensor("eb", [NE, D], F32, kind="ExternalInput").ap()
    gfm_h = nc.dram_tensor("gfm", [128, 8], F32, kind="ExternalInput").ap()
    bvfm_h = nc.dram_tensor("bvfm", [128, 8], F32, kind="ExternalInput").ap()
    gb4_h = nc.dram_tensor("gb4", [128, NSUB * NE], F32, kind="ExternalInput").ap()
    cst_h = nc.dram_tensor("cst", [128, 8], F32, kind="ExternalInput").ap()
    ident_h = nc.dram_tensor("ident", [128, 128], BF16, kind="ExternalInput").ap()
    ident32_h = nc.dram_tensor("ident32", [128, 128], F32, kind="ExternalInput").ap()
    ewb_h = nc.dram_tensor("ewb", [NE, D, D], BF16, kind="Internal").ap()
    out_h = nc.dram_tensor("out", [ntok, D], F32, kind="ExternalOutput").ap()
    if dbg:
        d_xh = nc.dram_tensor("d_xh", [128, NSUB * D], F32, kind="ExternalOutput").ap()
        d_xt = nc.dram_tensor("d_xt", [128, 8 * TT], F32, kind="ExternalOutput").ap()
        d_lgt = nc.dram_tensor("d_lgt", [128, NSUB * NE], F32, kind="ExternalOutput").ap()
        d_g8 = nc.dram_tensor("d_g8", [128, NSUB * NE], F32, kind="ExternalOutput").ap()
        d_r1 = nc.dram_tensor("d_r1", [128, NSUB * D], F32, kind="ExternalOutput").ap()
        d_sc32 = nc.dram_tensor("d_sc32", [128, NSUB * D], F32, kind="ExternalOutput").ap()
        d_sc32b = nc.dram_tensor("d_sc32b", [128, NSUB * D], F32, kind="ExternalOutput").ap()


    from contextlib import ExitStack
    es = ExitStack()
    sbt = lambda n, sh, dt=F32: es.enter_context(nc.sbuf_tensor(n, sh, dt))
    pst = lambda n, sh, dt=F32: es.enter_context(nc.psum_tensor(n, sh, dt))

    wfcT = sbt("wfcT_s", [128, 8 * D], BF16)
    wprojT = sbt("wprojT_s", [128, 8 * D], BF16)
    gwT = sbt("gwT_s", [128, 8 * NE], BF16)
    bB = sbt("bB_s", [NE, D], BF16)
    gfm = sbt("gfm_s", [128, 8])
    bvfm = sbt("bvfm_s", [128, 8])
    gb4 = sbt("gb4_s", [128, NSUB * NE])
    cst = sbt("cst_s", [128, 8])
    ident = sbt("ident_s", [128, 128], BF16)
    ident32 = sbt("ident32_s", [128, 128])
    negbig = sbt("negbig_s", [128, NSUB * NE])
    dmy = sbt("dmy_s", [128, 8])

    NWE = 3
    We = [sbt(f"We{b}", [128, 8 * D], BF16) for b in range(NWE)]
    X = [sbt(f"X{b}", [128, NSUB * D]) for b in range(2)]
    R1 = sbt("R1", [128, NSUB * D])
    OUT = [sbt("OUT0", [128, NSUB * D])]
    XH = sbt("XH", [128, NSUB * D], BF16)
    XH1 = sbt("XH1", [128, NSUB * D], BF16)
    XT = sbt("XT", [128, 8 * TT], BF16)
    RL = [sbt(f"RL{b}", [128, 512], BF16) for b in range(2)]
    Pp = [sbt(f"Pp{b}", [128, 512], BF16) for b in range(3)]
    gT = sbt("gT_s", [NE, TT], BF16)
    g8 = sbt("g8_s", [128, NSUB * NE])
    g8b = sbt("g8b_s", [128, NSUB * NE], BF16)

    lgt = sbt("lgt", [128, NSUB * NE])
    lmask = sbt("lmask", [128, NSUB * NE], U8)
    lp = sbt("lp", [128, NSUB * NE])
    keep = sbt("keep", [128, NSUB * NE])
    e8g = sbt("e8g", [128, NSUB * NE])
    ekg = sbt("ekg", [128, NSUB * NE])
    st_ = {n: sbt(n, [128, NSUB]) for n in
           ["ssum", "ssq", "mu", "musq", "var", "lnv", "rstd", "nm",
            "m1", "m1n", "m2", "zz", "zz2", "zinv", "r2c", "tmpv", "lnt"]}

    dbg_f32 = sbt("dbg_f32", [128, 8 * TT]) if dbg else None
    PM = [pst(f"PM{i}", [128, 512]) for i in range(6)]
    PT = [pst(f"PTb{i}", [128, 1024], BF16) for i in range(2)]

    C_1024INV, C_NEG1, C_TWO = 0, 1, 2

    p = Plan()
    free = {}
    tr = {"pt": 0}

    def bank_free_wait(bank):
        return free.pop(("PM", bank), None)

    def dma(eng, dst, src, sem, waits=()):
        return p.emit(eng, lambda e, d=dst, s=src: e.dma_start(d, s),
                      waits=waits, inc=sem, inc_by=16)

    dma("sync", cst[:, :], cst_h, "sQc")
    dma("sync", gfm[:, :], gfm_h, "sQc")
    dma("sync", bvfm[:, :], bvfm_h, "sQc")
    dma("sync", gb4[:, :], gb4_h, "sQc")
    dma("sync", ident[:, :], ident_h, "sQc")
    dma("sync", ident32[:, :], ident32_h, "sQc")
    q_consts = p.cnt["sQc"]
    qX = [("sQx0", dma("sync", X[0][:, :],
                       x_h[0:TT, :].rearrange("(s q) f -> q s f", q=128), "sQx0"))]

    dma("gpsimd", wfcT[:, :], wfcT_h.rearrange("(c q) i -> q c i", q=128), "sGs")
    dma("gpsimd", wprojT[:, :], wprojT_h.rearrange("(c q) i -> q c i", q=128), "sGs")
    dma("gpsimd", gwT[:, :], gwT_h.rearrange("(c q) e -> q c e", q=128), "sGs")
    g_static = dma("gpsimd", bB[:, :], eb_h, "sGs")

    p.emit("vector", lambda e: e.memset(negbig[:, :], -1e30))

    wload_seq = [(i, m, e) for i in range(n_tiles) for m in range(2) for e in range(NE)]
    qW, wfree = {}, {}
    wstate = {"emitted": 0}

    bstore_totals = {}

    def emit_wloads(upto):
        while wstate["emitted"] < upto:
            k = wstate["emitted"]
            tile, m, e = wload_seq[k]
            waits = []
            if k - NWE in wfree:
                waits.extend(wfree[k - NWE])
            sem = f"sGw{k % NWE}"
            if k < NE:
                q = dma("gpsimd", We[k % NWE][:, :],
                        ew_h[e].rearrange("(c q) o -> q c o", q=128), sem, waits=waits)
                bsem = f"sB{k % 2}"
                bq = dma("sync", ewb_h[e].rearrange("(c q) o -> q c o", q=128),
                         We[k % NWE][:, :], bsem, waits=[(sem, q)])
                bstore_totals[bsem] = bq
                wfree.setdefault(k, [])
                wfree[k].append((bsem, bq))
                qW[(tile, m, e)] = (sem, q)
            else:
                waits2 = list(waits)
                if k == NE:
                    for bs, bv in bstore_totals.items():
                        waits2.append((bs, bv))
                qW[(tile, m, e)] = (sem, dma(
                    "gpsimd", We[k % NWE][:, :],
                    ewb_h[e].rearrange("(c q) o -> q c o", q=128), sem, waits=waits2))
            wstate["emitted"] += 1

    def ln_stats_emit(src, trash, extra_waits=(), sub_waits=None):
        ssum, ssq = st_["ssum"], st_["ssq"]
        ew = tuple(extra_waits)

        def wsub(s):
            w = list(ew) if s == 0 else []
            if sub_waits is not None:
                w.append(sub_waits[s])
            return w

        for s in range(NSUB):
            p.emit("vector", lambda e, s=s: e.tensor_reduce(
                ssum[:, s:s + 1], src[:, s * D:(s + 1) * D], AX.X, ALU.add),
                waits=wsub(s))
        for s in range(NSUB):
            p.emit("scalar", lambda e, s=s: e.activation(
                trash[:, s * D:(s + 1) * D], src[:, s * D:(s + 1) * D],
                AF.Square, accum_out=ssq[:, s:s + 1]),
                waits=wsub(s))
        p.emit("scalar", lambda e: e.activation(dmy[:, :], cst[:, :], AF.Identity))
        return p.emit("scalar", lambda e: e.drain(), inc="sS")

    def ln_var_emit(ssq_rdy):
        ssum, ssq = st_["ssum"], st_["ssq"]
        mu, musq, var = st_["mu"], st_["musq"], st_["var"]
        lnv, rstd, nm = st_["lnv"], st_["rstd"], st_["nm"]
        p.emit("vector", lambda e: e.tensor_copy(dmy[:, :], cst[:, :]))
        p.emit("vector", lambda e: e.tensor_scalar(
            mu[:, :], ssum[:, :], cst[:, C_1024INV:C_1024INV + 1], None, ALU.mult))
        p.emit("vector", lambda e: e.tensor_copy(dmy[:, :], cst[:, :]))
        p.emit("vector", lambda e: e.tensor_tensor(musq[:, :], mu[:, :], mu[:, :], ALU.mult))
        p.emit("vector", lambda e: e.tensor_copy(dmy[:, :], cst[:, :]))
        p.emit("vector", lambda e: e.scalar_tensor_tensor(
            var[:, :], ssq[:, :], cst[:, C_1024INV:C_1024INV + 1], musq[:, :],
            ALU.mult, ALU.subtract), waits=[("sS", ssq_rdy)])
        var_rdy = p.emit("vector", lambda e: e.drain(), inc="sV")
        p.emit("scalar", lambda e: e.activation(lnv[:, :], var[:, :], AF.Ln,
                                                bias=np.float32(1e-5)),
               waits=[("sV", var_rdy)])
        rstd_rdy = p.emit("scalar", lambda e: e.activation(
            rstd[:, :], lnv[:, :], AF.Exp, scale=np.float32(-0.5)), inc="sS")
        nm_rdy = p.emit("vector", lambda e: e.scalar_tensor_tensor(
            nm[:, :], mu[:, :], cst[:, C_NEG1:C_NEG1 + 1], rstd[:, :],
            ALU.mult, ALU.mult), waits=[("sS", rstd_rdy)], inc="sV")
        return rstd_rdy, nm_rdy

    def ln_chain_emit(src, dst, ssq_rdy):
        rstd, nm = st_["rstd"], st_["nm"]
        _, nm_rdy = ln_var_emit(ssq_rdy)
        p.emit("vector", lambda e: e.tensor_copy(dmy[:, :], cst[:, :]))
        p.emit("vector", lambda e: e.tensor_copy(dmy[:, :], cst[:, :]))
        outs = []
        for s in range(NSUB):
            if s % 2 == 0:
                outs.append(("sS", p.emit("scalar", lambda e, s=s: e.activation(
                    dst[:, s * D:(s + 1) * D], src[:, s * D:(s + 1) * D],
                    AF.Identity, scale=rstd[:, s:s + 1], bias=nm[:, s:s + 1]),
                    waits=[("sV", nm_rdy)], inc="sS")))
            else:
                outs.append(("sV", p.emit("vector", lambda e, s=s: e.tensor_scalar(
                    dst[:, s * D:(s + 1) * D], src[:, s * D:(s + 1) * D],
                    rstd[:, s:s + 1], nm[:, s:s + 1], ALU.mult, ALU.add),
                    inc="sV")))
        return outs

    def ln_emit(src, dst, extra_waits=(), trash=None, sub_waits=None):
        if trash is None:
            trash = XH
        ssq_rdy = ln_stats_emit(src, trash, extra_waits, sub_waits)
        return ln_chain_emit(src, dst, ssq_rdy)

    def transpose_sub(src_bf16, s, sub_wait, affine=False):
        b = tr["pt"]
        tr["pt"] ^= 1
        waits = [sub_wait]
        fw = free.pop(("PT", b), None)
        if fw:
            waits.append(fw)
        v = None
        for c in range(8):
            v = p.emit("tensor", lambda e, s=s, c=c, b=b: e.transpose(
                PT[b][:, c * 128:(c + 1) * 128],
                src_bf16[:, s * D + c * 128:s * D + (c + 1) * 128], ident[:, :]),
                waits=waits if c == 0 else (),
                inc="sT" if c == 7 else None)
        if affine:
            done = None
            for c in range(8):
                done = p.emit("scalar", lambda e, c=c, s=s, b=b: e.activation(
                    XT[:, c * TT + 128 * s: c * TT + 128 * (s + 1)],
                    PT[b][:, c * 128:(c + 1) * 128],
                    AF.Identity, scale=gfm[:, c:c + 1], bias=bvfm[:, c:c + 1]),
                    waits=[("sT", v)] if c == 0 else (),
                    inc="sS" if c == 7 else None)
            free[("PT", b)] = ("sS", done)
            return [("sS", done)]
        dst = XT[:, :].rearrange("q (c t) -> q c t", t=TT)[:, :, 128 * s:128 * (s + 1)]
        srcv = PT[b][:, 0:1024].rearrange("q (c t) -> q c t", t=128)
        done = p.emit("vector", lambda e, d=dst, sv=srcv: e.tensor_copy(d, sv),
                      waits=[("sT", v)], inc="sV")
        free[("PT", b)] = ("sV", done)
        return [("sV", done)]

    def moe_emit(tile, m, xt_subs, res_out, res_in, res_out_waits=(), mid_hooks=None):
        lg_last = None
        for s in range(NSUB):
            waits = list(xt_subs[s])
            if s == 0:
                fw = bank_free_wait(5)
                if fw:
                    waits.append(fw)
            for c in range(8):
                lg_last = p.emit("tensor", lambda e, s=s, c=c: e.matmul(
                    PM[5][:, s * NE:(s + 1) * NE],
                    XT[:, c * TT + 128 * s: c * TT + 128 * (s + 1)],
                    gwT[:, c * NE:(c + 1) * NE],
                    start=(c == 0), stop=(c == 7)),
                    waits=waits if c == 0 else (),
                    inc="sT" if c == 7 else None)
        lgt_rdy = p.emit("vector", lambda e: e.scalar_tensor_tensor(
            lgt[:, :], PM[5][:, 0:NSUB * NE], cst[:, C_NEG1:C_NEG1 + 1],
            gb4[:, :], ALU.bypass, ALU.add),
            waits=[("sT", lg_last)], inc="sV")
        free[("PM", 5)] = ("sV", lgt_rdy)

        m1, m1n, m2 = st_["m1"], st_["m1n"], st_["m2"]
        zz, zz2, zinv = st_["zz"], st_["zz2"], st_["zinv"]
        for s in range(NSUB):
            p.emit("vector", lambda e, s=s: e.tensor_reduce(
                m1[:, s:s + 1], lgt[:, s * NE:(s + 1) * NE], AX.X, ALU.max))
        p.emit("vector", lambda e: e.tensor_copy(dmy[:, :], cst[:, :]))
        p.emit("vector", lambda e: e.tensor_scalar(
            m1n[:, :], m1[:, :], cst[:, C_NEG1:C_NEG1 + 1], None, ALU.mult))
        for s in range(NSUB):
            p.emit("vector", lambda e, s=s: e.tensor_scalar(
                lmask[:, s * NE:(s + 1) * NE], lgt[:, s * NE:(s + 1) * NE],
                m1[:, s:s + 1], None, ALU.is_lt))
        m1n_rdy = p.emit("vector", lambda e: e.drain(), inc="sV")
        for s in range(NSUB):
            p.emit("vector", lambda e, s=s: e.select(
                lp[:, s * NE:(s + 1) * NE], lmask[:, s * NE:(s + 1) * NE],
                lgt[:, s * NE:(s + 1) * NE], negbig[:, s * NE:(s + 1) * NE]))
        for s in range(NSUB):
            p.emit("vector", lambda e, s=s: e.tensor_reduce(
                m2[:, s:s + 1], lp[:, s * NE:(s + 1) * NE], AX.X, ALU.max))
        p.emit("vector", lambda e: e.tensor_copy(dmy[:, :], cst[:, :]))
        for s in range(NSUB):
            p.emit("vector", lambda e, s=s: e.tensor_scalar(
                keep[:, s * NE:(s + 1) * NE], lgt[:, s * NE:(s + 1) * NE],
                m2[:, s:s + 1], None, ALU.is_ge))
        keep_rdy = p.emit("vector", lambda e: e.drain(), inc="sV")
        exps = []
        for s in range(NSUB):
            exps.append(p.emit("scalar", lambda e, s=s: e.activation(
                e8g[:, s * NE:(s + 1) * NE], lgt[:, s * NE:(s + 1) * NE],
                AF.Exp, bias=m1n[:, s:s + 1]),
                waits=[("sV", m1n_rdy)], inc="sS"))
        p.emit("vector", lambda e: e.tensor_tensor(
            ekg[:, :], e8g[:, :], keep[:, :], ALU.mult),
            waits=[("sS", exps[-1]), ("sV", keep_rdy)])
        for s in range(NSUB):
            p.emit("vector", lambda e, s=s: e.tensor_reduce(
                zz[:, s:s + 1], ekg[:, s * NE:(s + 1) * NE], AX.X, ALU.add))
        p.emit("vector", lambda e: e.tensor_copy(dmy[:, :], cst[:, :]))
        p.emit("vector", lambda e: e.tensor_scalar(
            zz2[:, :], zz[:, :], cst[:, C_TWO:C_TWO + 1], None, ALU.mult))
        p.emit("vector", lambda e: e.tensor_copy(dmy[:, :], cst[:, :]))
        p.emit("vector", lambda e: e.reciprocal(zinv[:, :], zz2[:, :]))
        p.emit("vector", lambda e: e.tensor_copy(dmy[:, :], cst[:, :]))
        g8_last = None
        for s in range(NSUB):
            g8_last = p.emit("vector", lambda e, s=s: e.tensor_scalar(
                g8[:, s * NE:(s + 1) * NE], ekg[:, s * NE:(s + 1) * NE],
                zinv[:, s:s + 1], None, ALU.mult), inc="sV")

        def emit_gt_gb():
            p.emit("vector", lambda e: e.tensor_copy(dmy[:, :], cst[:, :]))
            g8b_rdy = p.emit("vector", lambda e: e.tensor_copy(g8b[:, :], g8[:, :]), inc="sV")
            b = tr["pt"]
            tr["pt"] ^= 1
            waits = [("sV", g8b_rdy)]
            fw = free.pop(("PT", b), None)
            if fw:
                waits.append(fw)
            v = None
            for s in range(NSUB):
                v = p.emit("tensor", lambda e, s=s, b=b: e.transpose(
                    PT[b][0:NE, 128 * s:128 * (s + 1)], g8b[:, s * NE:(s + 1) * NE],
                    ident[:, :]),
                    waits=waits if s == 0 else (),
                    inc="sT" if s == NSUB - 1 else None)
            gT_rdy = p.emit("scalar", lambda e, b=b: e.copy(gT[0:NE, :], PT[b][0:NE, 0:TT]),
                            waits=[("sT", v)], inc="sS")
            free[("PT", b)] = ("sS", gT_rdy)
            init_last = None
            for k in range(NSUB * 2):
                si, h = k // 2, k % 2
                bank = k % 2
                waits = [("sS", gT_rdy)] if k == 0 else []
                fw = bank_free_wait(bank)
                if fw:
                    waits.append(fw)
                v2 = p.emit("tensor", lambda e, si=si, h=h, bank=bank: e.matmul(
                    PM[bank][:, 0:512], gT[0:NE, 128 * si:128 * (si + 1)],
                    bB[0:NE, 512 * h:512 * (h + 1)], start=True, stop=True),
                    waits=waits, inc="sT")
                col = si * D + 512 * h
                iwaits = [("sT", v2)]
                if k == 0:
                    iwaits.extend(res_out_waits)
                init_last = p.emit("vector", lambda e, col=col, bank=bank: e.scalar_tensor_tensor(
                    res_out[:, col:col + 512], PM[bank][:, 0:512],
                    cst[:, C_NEG1:C_NEG1 + 1], res_in[:, col:col + 512],
                    ALU.bypass, ALU.add),
                    waits=iwaits, inc="sV")
                free[("PM", bank)] = ("sV", init_last)
            return init_last

        sub_done = {}
        ev = None
        pending = []
        flat = [(e_i, k) for e_i in range(NE) for k in range(NSUB * 2)]
        wq = wbuf = widx = None
        for idx, (e_i, k) in enumerate(flat):
            if k == 0:
                widx = wload_seq.index((tile, m, e_i))
                emit_wloads(widx + 1)
                wq = qW[(tile, m, e_i)]
                wbuf = widx % NWE
            si, h = k // 2, k % 2
            g2 = e_i * 8 + k
            bank = 2 + (g2 % 3)
            waits = list(xt_subs[si])
            if k == 0:
                waits.append(wq)
            fw = bank_free_wait(bank)
            if fw:
                waits.append(fw)
            last_grp = None
            for f in range(8):
                last_grp = p.emit("tensor", lambda e, si=si, h=h, f=f, bank=bank, wbuf=wbuf: e.matmul(
                    PM[bank][:, 0:512],
                    XT[:, f * TT + 128 * si: f * TT + 128 * (si + 1)],
                    We[wbuf][:, f * D + 512 * h: f * D + 512 * (h + 1)],
                    start=(f == 0), stop=(f == 7)),
                    waits=waits if f == 0 else (),
                    inc="sT" if f == 7 else None)
            col = si * D + 512 * h
            gcol = si * NE + e_i
            if idx < 2:
                pending.append((bank, col, gcol, last_grp))
            else:
                if idx == 2:
                    init_last = emit_gt_gb()
                    for (pbank, pcol, pgcol, pgrp) in pending:
                        ev = p.emit("vector", lambda e, col=pcol, bank=pbank, gcol=pgcol: e.scalar_tensor_tensor(
                            res_out[:, col:col + 512], PM[bank][:, 0:512],
                            g8[:, gcol:gcol + 1], res_out[:, col:col + 512],
                            ALU.mult, ALU.add),
                            waits=[("sT", pgrp), ("sV", init_last)], inc="sV")
                        free[("PM", pbank)] = ("sV", ev)
                    pending = []
                ev = p.emit("vector", lambda e, col=col, bank=bank, gcol=gcol: e.scalar_tensor_tensor(
                    res_out[:, col:col + 512], PM[bank][:, 0:512],
                    g8[:, gcol:gcol + 1], res_out[:, col:col + 512],
                    ALU.mult, ALU.add),
                    waits=[("sT", last_grp)], inc="sV")
                free[("PM", bank)] = ("sV", ev)
            if mid_hooks and idx in mid_hooks:
                mid_hooks[idx]()
            if k == NSUB * 2 - 1:
                wfree.setdefault(widx, [])
                wfree[widx].append(("sT", last_grp))
            if e_i == NE - 1 and h == 1:
                sub_done[si] = ev
        done = p.emit("vector", lambda e: e.drain(), waits=[("sV", ev)], inc="sV")
        return done, sub_done

    def attn_emit(tile, xb, ob_other, r1_done, sc32_wait, r1_subs=None):
        sw = None if r1_subs is None else [("sV", r1_subs[s]) for s in range(NSUB)]
        ew = [("sV", r1_done)] if r1_subs is None else []
        ssq_rdy = ln_stats_emit(R1, XH, extra_waits=ew, sub_waits=sw)
        rstd_rdy, _ = ln_var_emit(ssq_rdy)
        var, rstd, mu = st_["var"], st_["rstd"], st_["mu"]
        musq = st_["musq"]
        tmpv, lnt, r2c = st_["tmpv"], st_["lnt"], st_["r2c"]
        zz2 = st_["zz2"]
        p.emit("vector", lambda e: e.tensor_tensor(musq[:, :], var[:, :], rstd[:, :], ALU.mult),
               waits=[("sS", rstd_rdy)])
        p.emit("vector", lambda e: e.tensor_copy(dmy[:, :], cst[:, :]))
        tv = p.emit("vector", lambda e: e.tensor_tensor(tmpv[:, :], musq[:, :], rstd[:, :], ALU.mult), inc="sV")
        p.emit("scalar", lambda e: e.activation(lnt[:, :], tmpv[:, :], AF.Ln,
                                                bias=np.float32(1e-5)),
               waits=[("sV", tv)])
        p.emit("scalar", lambda e: e.activation(
            r2c[:, :], lnt[:, :], AF.Exp, scale=np.float32(-0.5)))
        iv_rdy = p.emit("scalar", lambda e: e.activation(
            st_["zinv"][:, :], lnt[:, :], AF.Exp, scale=np.float32(0.5)), inc="sS")
        p.emit("vector", lambda e: e.tensor_tensor(zz2[:, :], rstd[:, :], r2c[:, :], ALU.mult),
               waits=[("sS", iv_rdy)])
        p.emit("vector", lambda e: e.tensor_copy(dmy[:, :], cst[:, :]))
        nm2_rdy = p.emit("vector", lambda e: e.scalar_tensor_tensor(
            st_["nm"][:, :], mu[:, :], cst[:, C_NEG1:C_NEG1 + 1], zz2[:, :],
            ALU.mult, ALU.mult), inc="sV")
        subs = []
        for s in range(NSUB):
            subs.append(p.emit("scalar", lambda e, s=s: e.activation(
                XH[:, s * D:(s + 1) * D], R1[:, s * D:(s + 1) * D],
                AF.Identity, scale=zz2[:, s:s + 1], bias=st_["nm"][:, s:s + 1]),
                waits=[("sV", nm2_rdy)], inc="sS"))
        inv2 = st_["zinv"]
        for s in range(NSUB):
            p.emit("vector", lambda e, s=s: e.scalar_tensor_tensor(
                R1[:, s * D:(s + 1) * D], XH[:, s * D:(s + 1) * D],
                inv2[:, s:s + 1], R1[:, s * D:(s + 1) * D],
                ALU.mult, ALU.add),
                waits=[("sS", subs[s])], inc="sV")
        xt_rdy = None
        for s in range(NSUB):
            xt_rdy = transpose_sub(XH, s, ("sS", subs[s]), affine=True)
        if dbg == "B0" and tile == 0:
            fold_done = p.emit("vector", lambda e: e.drain(), inc="sV")
            dma("sync", d_r1, R1[:, :], "sO0", waits=[("sV", fold_done)])
            cv1 = p.emit("vector", lambda e: e.tensor_copy(dbg_f32[:, 0:NSUB * D], XH[:, :]),
                         waits=[("sV", a_sub[-1])], inc="sV")
            dma("sync", d_xh, dbg_f32[:, 0:NSUB * D], "sO0", waits=[("sV", cv1)])
            cv2 = p.emit("vector", lambda e: e.tensor_copy(dbg_f32[:, :], XT[:, :]),
                         waits=[("sO0", p.cnt["sO0"]), ("sV", cv1)], inc="sV")
            dma("sync", d_xt, dbg_f32[:, :], "sO0", waits=[("sV", cv2)])
            dma("sync", d_sc32, SC32[:, :], "sO0", waits=[("sV", cv2)])
            dma("sync", d_sc32b, SC32B[:, :], "sO0", waits=[("sV", cv2)])
            return None

        relu_cnt, sq_cnt, stage2_done = {}, {}, {}
        out_evac = {}

        def emit_stage2(h, g, k):
            if g in stage2_done:
                return
            jj, ic = k // 8, k % 8
            v = None
            for s in range(NSUB):
                waits = []
                if s == 0:
                    waits.append(("sV", sq_cnt[g]))
                if k == 0:
                    fw = bank_free_wait(s)
                    if fw:
                        waits.append(fw)
                v = p.emit("tensor", lambda e, s=s, jj=jj, ic=ic, g=g, k=k: e.matmul(
                    PM[s][:, 0:256],
                    Pp[g % 3][:, 128 * s:128 * (s + 1)],
                    wprojT[:, ic * D + 256 * jj: ic * D + 256 * (jj + 1)],
                    start=(k == 0), stop=(k == 31)),
                    waits=waits, inc="sT" if s == NSUB - 1 else None)
            stage2_done[g] = v

        g_global = 0
        nh_run = 1 if dbg in ("B3", "B4", "B5") else NH
        nj_run = 1 if dbg in ("B4", "B5") else 4
        ni_run = 1 if dbg == "B4" else 8
        for h in range(nh_run):
            for jj in range(nj_run):
                for ic in range(ni_run):
                    g = g_global
                    g_global += 1
                    k = jj * 8 + ic
                    sbank = 4 + (g % 2)
                    waits = []
                    if g == 0:
                        waits.extend(xt_rdy)
                    if g >= 2:
                        waits.append(("sS", relu_cnt[g - 2]))
                    if g < 2:
                        fw = bank_free_wait(sbank)
                        if fw:
                            waits.append(fw)
                    v = None
                    for dc in range(2):
                        cc_w = 2 * jj + dc
                        cc_a = 2 * h + dc
                        v = p.emit("tensor", lambda e, cc_w=cc_w, cc_a=cc_a, ic=ic, sbank=sbank, dc=dc: e.matmul(
                            PM[sbank][:, 0:512],
                            wfcT[:, cc_w * D + 128 * ic: cc_w * D + 128 * (ic + 1)],
                            XT[:, cc_a * TT:(cc_a + 1) * TT],
                            start=(dc == 0), stop=(dc == 1)),
                            waits=waits if dc == 0 else (),
                            inc="sT" if dc == 1 else None)
                    rwaits = [("sT", v)]
                    if g >= 2:
                        rwaits.append(("sV", sq_cnt[g - 2]))
                    relu_cnt[g] = p.emit("scalar", lambda e, g=g, sbank=sbank: e.activation(
                        RL[g % 2][:, :], PM[sbank][:, 0:512], AF.Relu),
                        waits=rwaits, inc="sS")
                    qwaits = [("sS", relu_cnt[g])]
                    if g >= 3:
                        qwaits.append(("sT", stage2_done[g - 3]))
                    sq_cnt[g] = p.emit("vector", lambda e, g=g: e.tensor_tensor(
                        Pp[g % 3][:, :], RL[g % 2][:, :], RL[g % 2][:, :], ALU.mult),
                        waits=qwaits, inc="sV")
                    if dbg == "B2" and tile == 0 and g == 0:
                        cv = p.emit("vector", lambda e: e.tensor_copy(
                            dbg_f32[:, 0:512], PM[4][:, 0:512]),
                            waits=[("sS", relu_cnt[0])], inc="sV")
                        cv2 = p.emit("vector", lambda e: e.tensor_copy(
                            dbg_f32[:, 512:1024], RL[0][:, :]), inc="sV")
                        cv3 = p.emit("vector", lambda e: e.tensor_copy(
                            dbg_f32[:, 1024:1536], Pp[0][:, :]),
                            waits=[("sV", sq_cnt[0])], inc="sV")
                        cv4 = p.emit("vector", lambda e: e.tensor_copy(
                            dbg_f32[:, 1536:2048], wfcT[:, 0:512]), inc="sV")
                        cv5 = p.emit("vector", lambda e: e.tensor_copy(
                            dbg_f32[:, 2048:2560], XT[:, 0:512]), inc="sV")
                        cv6 = p.emit("vector", lambda e: e.tensor_copy(
                            dbg_f32[:, 2560:3072], XT[:, 512:1024]),
                            waits=[("sV", cv3)], inc="sV")
                        dma("sync", d_xt, dbg_f32[:, :], "sO0", waits=[("sV", cv6)])
                        return None
                    if k >= 2:
                        emit_stage2(h, g - 2, k - 2)
            if dbg == "B4":
                emit_stage2(h, 0, 0)
            elif dbg == "B5":
                emit_stage2(h, g_global - 2, 6)
                emit_stage2(h, g_global - 1, 7)
            else:
                emit_stage2(h, g_global - 2, 30)
                emit_stage2(h, g_global - 1, 31)
            for s in range(NSUB):
                evv = p.emit("vector", lambda e, s=s, h=h: e.scalar_tensor_tensor(
                    R1[:, s * D + 256 * h: s * D + 256 * (h + 1)],
                    PM[s][:, 0:256], cst[:, C_NEG1:C_NEG1 + 1],
                    R1[:, s * D + 256 * h: s * D + 256 * (h + 1)],
                    ALU.bypass, ALU.add),
                    waits=[("sT", stage2_done[g_global - 1])], inc="sV")
                out_evac[(h, s)] = evv
                free[("PM", s)] = ("sV", evv)
        if g_global >= 2:
            free[("PM", 4)] = ("sS", relu_cnt[g_global - 2])
            free[("PM", 5)] = ("sS", relu_cnt[g_global - 1])
        x3_subs = {s: out_evac[(nh_run - 1, s)] for s in range(NSUB)}
        return p.emit("vector", lambda e: e.drain(),
                      waits=[("sV", out_evac[(nh_run - 1, NSUB - 1)])], inc="sV"), x3_subs

    x_scratch_done = {}
    out_free = {}
    subs_next = None
    for i in range(n_tiles):
        xb = i % 2
        ob = 0
        if i + 1 < n_tiles:
            waits = []
            if (i - 1) in x_scratch_done:
                waits.append(x_scratch_done[i - 1])
            sem = f"sQx{(i + 1) % 2}"
            qX.append((sem, dma("sync", X[(i + 1) % 2][:, :],
                                x_h[TT * (i + 1):TT * (i + 2), :].rearrange("(s q) f -> q s f", q=128),
                                sem, waits=waits)))
        if subs_next is None:
            subs = ln_emit(X[xb], XH1, extra_waits=[qX[i]], trash=XH1)
        else:
            subs = subs_next
        xt_subs = [transpose_sub(XH1, s, subs[s]) for s in range(NSUB)]
        r1_done, r1_subs = moe_emit(i, 0, xt_subs, R1, X[xb])
        if dbg == "A" and i == 0:
            cv1 = p.emit("vector", lambda e: e.tensor_copy(dbg_f32[:, 0:NSUB * D], XH[:, :]),
                         waits=[("sV", r1_done)], inc="sV")
            dma("sync", d_xh, dbg_f32[:, 0:NSUB * D], "sO0", waits=[("sV", cv1)])
            cv2 = p.emit("vector", lambda e: e.tensor_copy(dbg_f32[:, :], XT[:, :]),
                         waits=[("sO0", p.cnt["sO0"]), ("sV", cv1)], inc="sV")
            dma("sync", d_xt, dbg_f32[:, :], "sO0", waits=[("sV", cv2)])
            dma("sync", d_lgt, lgt[:, :], "sO0", waits=[("sV", r1_done)])
            dma("sync", d_g8, g8[:, :], "sO0", waits=[("sV", r1_done)])
            dma("sync", d_r1, R1[:, :], "sO0", waits=[("sV", r1_done)])
            break
        x3_res = attn_emit(i, xb, 1 - ob, r1_done, out_free.get(1 - ob),
                           r1_subs=r1_subs)
        if x3_res is None:
            break
        x3_done, x3_subs = x3_res
        if dbg in ("B3", "B4", "B5") and i == 0:
            dma("sync", d_r1, R1[:, :], "sO0", waits=[("sV", x3_done)])
            if dbg in ("B4", "B5"):
                cvp = p.emit("vector", lambda e: e.tensor_copy(
                    dbg_f32[:, 0:512], Pp[0][:, :]), waits=[("sV", x3_done)], inc="sV")
                cvr = p.emit("vector", lambda e: e.tensor_copy(
                    dbg_f32[:, 512:1024], RL[0][:, :]), inc="sV")
                cvs = p.emit("vector", lambda e: e.tensor_copy(
                    dbg_f32[:, 1024:1536], PM[4][:, 0:512]), inc="sV")
                cvs = p.emit("vector", lambda e: e.tensor_copy(
                    dbg_f32[:, 1536:2048], PM[5][:, 0:512]), inc="sV")
                dma("sync", d_xt, dbg_f32[:, :], "sO0", waits=[("sV", cvs)])
            break
        if dbg == "C0" and i == 0:
            dma("sync", d_r1, R1[:, :], "sO0", waits=[("sV", x3_done)])
        if dbg == "B0" and i == 0:
            break
        x_scratch_done[i] = ("sV", x3_done)
        if dbg == "B" and i == 0:
            dma("sync", d_r1, R1[:, :], "sO0", waits=[("sV", x3_done)])
            break
        if dbg == "C4" and i == 0:
            for _ in range(8):
                p.emit("vector", lambda e: e.tensor_copy(dmy[:, :], cst[:, :]))
                p.emit("scalar", lambda e: e.activation(dmy[:, :], cst[:, :], AF.Identity))
            dd = p.emit("vector", lambda e: e.drain(), waits=[("sV", x3_done)], inc="sV")
            dma("sync", d_r1, R1[:, :], "sO0", waits=[("sV", dd)])
            break
        subs3 = ln_emit(R1, XH, sub_waits=[("sV", x3_subs[s]) for s in range(NSUB)])
        if dbg == "C1" and i == 0:
            dma("sync", d_r1, R1[:, :], "sO0", waits=[("sS", subs3[-1])])
            break
        xt3 = [transpose_sub(XH, s, subs3[s]) for s in range(NSUB)]
        if dbg == "C2" and i == 0:
            dma("sync", d_r1, R1[:, :], "sO0", waits=list(xt3[-1]))
            break
        ow = [out_free[ob]] if ob in out_free else []
        holder = {}

        def ln1_stats(i=i, holder=holder):
            if i + 1 < n_tiles and dbg is None:
                holder["st"] = ln_stats_emit(X[(i + 1) % 2], XH1,
                                             extra_waits=[qX[i + 1]])

        def ln1_rest(i=i, holder=holder):
            if i + 1 < n_tiles and dbg is None:
                holder["subs"] = ln_chain_emit(X[(i + 1) % 2], XH1, holder["st"])

        out_done, _ = moe_emit(i, 1, xt3, OUT[ob], R1, res_out_waits=ow,
                               mid_hooks={1: ln1_stats, 13: ln1_rest})
        subs_next = holder.get("subs")
        if dbg == "C" and i == 0:
            cv1 = p.emit("vector", lambda e: e.tensor_copy(dbg_f32[:, 0:NSUB * D], XH[:, :]),
                         waits=[("sV", out_done)], inc="sV")
            dma("sync", d_xh, dbg_f32[:, 0:NSUB * D], "sO0", waits=[("sV", cv1)])
            cv2 = p.emit("vector", lambda e: e.tensor_copy(dbg_f32[:, :], XT[:, :]),
                         waits=[("sO0", p.cnt["sO0"]), ("sV", cv1)], inc="sV")
            dma("sync", d_xt, dbg_f32[:, :], "sO0", waits=[("sV", cv2)])
            dma("sync", d_lgt, lgt[:, :], "sO0", waits=[("sV", out_done)])
            dma("sync", d_g8, g8[:, :], "sO0", waits=[("sV", out_done)])
            dma("sync", d_r1, R1[:, :], "sO0", waits=[("sV", out_done)])
            dma("sync", d_sc32, OUT[ob][:, :], "sO0", waits=[("sV", out_done)])
            break
        dst = out_h[TT * i:TT * (i + 1), :].rearrange("(s q) f -> q s f", q=128)
        sem = f"sO{ob}"
        sq_ = dma("sync", dst, OUT[ob][:, :].rearrange("q (s f) -> q s f", f=D),
                  sem, waits=[("sV", out_done)])
        out_free[ob] = (sem, sq_)

    final_stores = [("sO0", p.cnt["sO0"]), ("sO1", p.cnt["sO1"])]

    sems = {s: es.enter_context(nc.semaphore(s))
            for s in ["sT", "sV", "sS", "sQc", "sQx0", "sQx1",
                      "sGs", "sGw0", "sGw1", "sGw2", "sO0", "sO1", "sB0", "sB1"]}
    block = es.enter_context(nc.Block())

    def mk(name):
        def f(eng):
            if name in ("tensor", "vector", "scalar"):
                eng.wait_ge(sems["sQc"], q_consts)
                eng.wait_ge(sems["sGs"], g_static)
            p.replay(name, eng, sems)
            if name == "sync":
                for s_, v_ in final_stores:
                    if v_:
                        eng.wait_ge(sems[s_], v_)
        return f

    block.sync(mk("sync"))
    block.gpsimd(mk("gpsimd"))
    block.vector(mk("vector"))
    block.scalar(mk("scalar"))
    block.tensor(mk("tensor"))

    es.close()
    return nc


def make_common(w_fc, w_proj, ln_g, ln_b, gate_w, gate_b, expert_w, expert_b):
    return {
        "wfcT": np.ascontiguousarray(np.asarray(w_fc, dtype=np.float32).T),
        "wprojT": np.ascontiguousarray(np.asarray(w_proj, dtype=np.float32).T),
        "gwT": np.ascontiguousarray(np.asarray(gate_w, dtype=np.float32).T),
        "ew": np.ascontiguousarray(np.asarray(expert_w, dtype=np.float32)),
        "eb": np.ascontiguousarray(np.asarray(expert_b, dtype=np.float32)),
        "gfm": np.ascontiguousarray(np.asarray(ln_g, dtype=np.float32).reshape(8, 128).T),
        "bvfm": np.ascontiguousarray(np.asarray(ln_b, dtype=np.float32).reshape(8, 128).T),
        "gb4": np.ascontiguousarray(np.tile(
            np.asarray(gate_b, dtype=np.float32), (128, NSUB))),
        "cst": np.ascontiguousarray(np.tile(np.array(
            [1.0 / D, -1.0, 2.0, 0, 0, 0, 0, 0], dtype=np.float32), (128, 1))),
        "ident": np.eye(128, dtype=ml_dtypes.bfloat16),
        "ident32": np.eye(128, dtype=np.float32),
    }


def kernel(L0_attn_residual, L0_attn_w_fc, L0_attn_w_proj, L0_attn_ln_g,
           L0_attn_ln_b, L0_ffn_gate_w, L0_ffn_gate_b, L0_ffn_expert_w,
           L0_ffn_expert_b):
    xfull = np.ascontiguousarray(
        np.asarray(L0_attn_residual, dtype=np.float32).reshape(-1, D))
    per = xfull.shape[0] // NCORE
    n_tiles = per // TT
    common = make_common(L0_attn_w_fc, L0_attn_w_proj, L0_attn_ln_g,
                         L0_attn_ln_b, L0_ffn_gate_w, L0_ffn_gate_b,
                         L0_ffn_expert_w, L0_ffn_expert_b)
    nc = build_program(n_tiles)
    in_maps = [dict(common, x=np.ascontiguousarray(xfull[c * per:(c + 1) * per]))
               for c in range(NCORE)]
    res = run_bass_kernel_spmd(nc, in_maps, core_ids=list(range(NCORE)))
    out = np.concatenate([res.results[c]["out"] for c in range(NCORE)], axis=0)
    return out.reshape(np.asarray(L0_attn_residual).shape).astype(np.float32)


# version 18 (final)
# speedup vs baseline: 1.0281x; 1.0281x over previous
import numpy as np
import ml_dtypes
import concourse.bass as bass
import concourse.mybir as mybir
from concourse.bass_utils import run_bass_kernel_spmd

F32 = mybir.dt.float32
BF16 = mybir.dt.bfloat16
U8 = mybir.dt.uint8
AF = mybir.ActivationFunctionType
ALU = mybir.AluOpType
AX = mybir.AxisListType

D = 1024
NSUB = 4
TT = 128 * NSUB
NE = 8
NH = 4
NCORE = 8


class Plan:

    ENGS = ("tensor", "vector", "scalar", "sync", "gpsimd")

    def __init__(self):
        self.ops = {e: [] for e in self.ENGS}
        self.cnt = {s: 0 for s in ("sT", "sV", "sS", "sQc", "sQx0", "sQx1",
                                    "sGs", "sGw0", "sGw1", "sGw2", "sO0", "sO1",
                                    "sB0", "sB1")}

    def emit(self, eng, fn, waits=(), inc=None, inc_by=1):
        for s, v in waits:
            assert v <= self.cnt[s], f"wait {s}>={v} but only {self.cnt[s]} emitted"
        if inc is not None:
            self.cnt[inc] += inc_by
        self.ops[eng].append((fn, tuple(waits), inc, inc_by))
        return self.cnt[inc] if inc is not None else None

    def replay(self, eng_name, eng, sems):
        last = {}
        for fn, waits, inc, inc_by in self.ops[eng_name]:
            for s, v in waits:
                if last.get(s, -1) < v:
                    eng.wait_ge(sems[s], v)
                    last[s] = v
            inst = fn(eng)
            if inc is not None:
                inst.then_inc(sems[inc], inc_by)


def build_program(n_tiles, dbg=None):
    nc = bass.Bass("TRN2", target_bir_lowering=False, detect_race_conditions=False)
    ntok = TT * n_tiles

    x_h = nc.dram_tensor("x", [ntok, D], F32, kind="ExternalInput").ap()
    wfcT_h = nc.dram_tensor("wfcT", [D, D], F32, kind="ExternalInput").ap()
    wprojT_h = nc.dram_tensor("wprojT", [D, D], F32, kind="ExternalInput").ap()
    gwT_h = nc.dram_tensor("gwT", [D, NE], F32, kind="ExternalInput").ap()
    ew_h = nc.dram_tensor("ew", [NE, D, D], F32, kind="ExternalInput").ap()
    eb_h = nc.dram_tensor("eb", [NE, D], F32, kind="ExternalInput").ap()
    gfm_h = nc.dram_tensor("gfm", [128, 8], F32, kind="ExternalInput").ap()
    bvfm_h = nc.dram_tensor("bvfm", [128, 8], F32, kind="ExternalInput").ap()
    gb4_h = nc.dram_tensor("gb4", [128, NSUB * NE], F32, kind="ExternalInput").ap()
    cst_h = nc.dram_tensor("cst", [128, 8], F32, kind="ExternalInput").ap()
    ident_h = nc.dram_tensor("ident", [128, 128], BF16, kind="ExternalInput").ap()
    ident32_h = nc.dram_tensor("ident32", [128, 128], F32, kind="ExternalInput").ap()
    ewb_h = nc.dram_tensor("ewb", [NE, D, D], BF16, kind="Internal").ap()
    out_h = nc.dram_tensor("out", [ntok, D], F32, kind="ExternalOutput").ap()
    if dbg:
        d_xh = nc.dram_tensor("d_xh", [128, NSUB * D], F32, kind="ExternalOutput").ap()
        d_xt = nc.dram_tensor("d_xt", [128, 8 * TT], F32, kind="ExternalOutput").ap()
        d_lgt = nc.dram_tensor("d_lgt", [128, NSUB * NE], F32, kind="ExternalOutput").ap()
        d_g8 = nc.dram_tensor("d_g8", [128, NSUB * NE], F32, kind="ExternalOutput").ap()
        d_r1 = nc.dram_tensor("d_r1", [128, NSUB * D], F32, kind="ExternalOutput").ap()
        d_sc32 = nc.dram_tensor("d_sc32", [128, NSUB * D], F32, kind="ExternalOutput").ap()
        d_sc32b = nc.dram_tensor("d_sc32b", [128, NSUB * D], F32, kind="ExternalOutput").ap()


    from contextlib import ExitStack
    es = ExitStack()
    sbt = lambda n, sh, dt=F32: es.enter_context(nc.sbuf_tensor(n, sh, dt))
    pst = lambda n, sh, dt=F32: es.enter_context(nc.psum_tensor(n, sh, dt))

    wfcT = sbt("wfcT_s", [128, 8 * D], BF16)
    wprojT = sbt("wprojT_s", [128, 8 * D], BF16)
    gwT = sbt("gwT_s", [128, 8 * NE], BF16)
    bB = sbt("bB_s", [NE, D], BF16)
    gfm = sbt("gfm_s", [128, 8])
    bvfm = sbt("bvfm_s", [128, 8])
    gb4 = sbt("gb4_s", [128, NSUB * NE])
    cst = sbt("cst_s", [128, 8])
    ident = sbt("ident_s", [128, 128], BF16)
    ident32 = sbt("ident32_s", [128, 128])
    negbig = sbt("negbig_s", [128, NSUB * NE])
    dmy = sbt("dmy_s", [128, 8])

    NWE = 3
    We = [sbt(f"We{b}", [128, 8 * D], BF16) for b in range(NWE)]
    X = [sbt(f"X{b}", [128, NSUB * D]) for b in range(2)]
    R1 = sbt("R1", [128, NSUB * D])
    OUT = [sbt("OUT0", [128, NSUB * D])]
    XH = sbt("XH", [128, NSUB * D], BF16)
    XH1 = sbt("XH1", [128, NSUB * D], BF16)
    XT = sbt("XT", [128, 8 * TT], BF16)
    RL = [sbt(f"RL{b}", [128, 512], BF16) for b in range(2)]
    Pp = [sbt(f"Pp{b}", [128, 512], BF16) for b in range(3)]
    gT = sbt("gT_s", [NE, TT], BF16)
    g8 = sbt("g8_s", [128, NSUB * NE])
    g8b = sbt("g8b_s", [128, NSUB * NE], BF16)

    lgt = sbt("lgt", [128, NSUB * NE])
    lmask = sbt("lmask", [128, NSUB * NE], U8)
    lp = sbt("lp", [128, NSUB * NE])
    keep = sbt("keep", [128, NSUB * NE])
    e8g = sbt("e8g", [128, NSUB * NE])
    ekg = sbt("ekg", [128, NSUB * NE])
    st_ = {n: sbt(n, [128, NSUB]) for n in
           ["ssum", "ssq", "mu", "musq", "var", "lnv", "rstd", "nm",
            "m1", "m1n", "m2", "zz", "zz2", "zinv", "r2c", "tmpv", "lnt"]}

    dbg_f32 = sbt("dbg_f32", [128, 8 * TT]) if dbg else None
    PM = [pst(f"PM{i}", [128, 512]) for i in range(6)]
    PT = [pst(f"PTb{i}", [128, 1024], BF16) for i in range(2)]

    C_1024INV, C_NEG1, C_TWO = 0, 1, 2

    p = Plan()
    free = {}
    tr = {"pt": 0}

    def bank_free_wait(bank):
        return free.pop(("PM", bank), None)

    def dma(eng, dst, src, sem, waits=()):
        return p.emit(eng, lambda e, d=dst, s=src: e.dma_start(d, s),
                      waits=waits, inc=sem, inc_by=16)

    dma("sync", cst[:, :], cst_h, "sQc")
    dma("sync", gfm[:, :], gfm_h, "sQc")
    dma("sync", bvfm[:, :], bvfm_h, "sQc")
    dma("sync", gb4[:, :], gb4_h, "sQc")
    dma("sync", ident[:, :], ident_h, "sQc")
    dma("sync", ident32[:, :], ident32_h, "sQc")
    q_consts = p.cnt["sQc"]
    qX = [("sQx0", dma("sync", X[0][:, :],
                       x_h[0:TT, :].rearrange("(s q) f -> q s f", q=128), "sQx0"))]

    dma("gpsimd", wfcT[:, :], wfcT_h.rearrange("(c q) i -> q c i", q=128), "sGs")
    dma("gpsimd", wprojT[:, :], wprojT_h.rearrange("(c q) i -> q c i", q=128), "sGs")
    dma("gpsimd", gwT[:, :], gwT_h.rearrange("(c q) e -> q c e", q=128), "sGs")
    g_static = dma("gpsimd", bB[:, :], eb_h, "sGs")

    p.emit("vector", lambda e: e.memset(negbig[:, :], -1e30))

    wload_seq = [(i, m, e) for i in range(n_tiles) for m in range(2) for e in range(NE)]
    qW, wfree = {}, {}
    wstate = {"emitted": 0}

    bstore_totals = {}

    def emit_wloads(upto):
        while wstate["emitted"] < upto:
            k = wstate["emitted"]
            tile, m, e = wload_seq[k]
            waits = []
            if k - NWE in wfree:
                waits.extend(wfree[k - NWE])
            sem = f"sGw{k % NWE}"
            if k < NE:
                q = dma("gpsimd", We[k % NWE][:, :],
                        ew_h[e].rearrange("(c q) o -> q c o", q=128), sem, waits=waits)
                bsem = f"sB{k % 2}"
                bq = dma("sync", ewb_h[e].rearrange("(c q) o -> q c o", q=128),
                         We[k % NWE][:, :], bsem, waits=[(sem, q)])
                bstore_totals[bsem] = bq
                wfree.setdefault(k, [])
                wfree[k].append((bsem, bq))
                qW[(tile, m, e)] = (sem, q)
            else:
                waits2 = list(waits)
                if k == NE:
                    for bs, bv in bstore_totals.items():
                        waits2.append((bs, bv))
                qW[(tile, m, e)] = (sem, dma(
                    "gpsimd", We[k % NWE][:, :],
                    ewb_h[e].rearrange("(c q) o -> q c o", q=128), sem, waits=waits2))
            wstate["emitted"] += 1

    def ln_stats_emit(src, trash, extra_waits=(), sub_waits=None):
        ssum, ssq = st_["ssum"], st_["ssq"]
        ew = tuple(extra_waits)

        def wsub(s):
            w = list(ew) if s == 0 else []
            if sub_waits is not None:
                w.append(sub_waits[s])
            return w

        for s in range(NSUB):
            p.emit("vector", lambda e, s=s: e.tensor_reduce(
                ssum[:, s:s + 1], src[:, s * D:(s + 1) * D], AX.X, ALU.add),
                waits=wsub(s))
        for s in range(NSUB):
            p.emit("scalar", lambda e, s=s: e.activation(
                trash[:, s * D:(s + 1) * D], src[:, s * D:(s + 1) * D],
                AF.Square, accum_out=ssq[:, s:s + 1]),
                waits=wsub(s))
        p.emit("scalar", lambda e: e.activation(dmy[:, :], cst[:, :], AF.Identity))
        return p.emit("scalar", lambda e: e.drain(), inc="sS")

    def ln_var_emit(ssq_rdy):
        ssum, ssq = st_["ssum"], st_["ssq"]
        mu, musq, var = st_["mu"], st_["musq"], st_["var"]
        lnv, rstd, nm = st_["lnv"], st_["rstd"], st_["nm"]
        p.emit("vector", lambda e: e.tensor_copy(dmy[:, :], cst[:, :]))
        p.emit("vector", lambda e: e.tensor_scalar(
            mu[:, :], ssum[:, :], cst[:, C_1024INV:C_1024INV + 1], None, ALU.mult))
        p.emit("vector", lambda e: e.tensor_copy(dmy[:, :], cst[:, :]))
        p.emit("vector", lambda e: e.tensor_tensor(musq[:, :], mu[:, :], mu[:, :], ALU.mult))
        p.emit("vector", lambda e: e.tensor_copy(dmy[:, :], cst[:, :]))
        p.emit("vector", lambda e: e.scalar_tensor_tensor(
            var[:, :], ssq[:, :], cst[:, C_1024INV:C_1024INV + 1], musq[:, :],
            ALU.mult, ALU.subtract), waits=[("sS", ssq_rdy)])
        var_rdy = p.emit("vector", lambda e: e.drain(), inc="sV")
        p.emit("scalar", lambda e: e.activation(lnv[:, :], var[:, :], AF.Ln,
                                                bias=np.float32(1e-5)),
               waits=[("sV", var_rdy)])
        rstd_rdy = p.emit("scalar", lambda e: e.activation(
            rstd[:, :], lnv[:, :], AF.Exp, scale=np.float32(-0.5)), inc="sS")
        nm_rdy = p.emit("vector", lambda e: e.scalar_tensor_tensor(
            nm[:, :], mu[:, :], cst[:, C_NEG1:C_NEG1 + 1], rstd[:, :],
            ALU.mult, ALU.mult), waits=[("sS", rstd_rdy)], inc="sV")
        return rstd_rdy, nm_rdy

    def ln_chain_emit(src, dst, ssq_rdy):
        rstd, nm = st_["rstd"], st_["nm"]
        _, nm_rdy = ln_var_emit(ssq_rdy)
        p.emit("vector", lambda e: e.tensor_copy(dmy[:, :], cst[:, :]))
        p.emit("vector", lambda e: e.tensor_copy(dmy[:, :], cst[:, :]))
        outs = []
        for s in range(NSUB):
            if s % 2 == 0:
                outs.append(("sS", p.emit("scalar", lambda e, s=s: e.activation(
                    dst[:, s * D:(s + 1) * D], src[:, s * D:(s + 1) * D],
                    AF.Identity, scale=rstd[:, s:s + 1], bias=nm[:, s:s + 1]),
                    waits=[("sV", nm_rdy)], inc="sS")))
            else:
                outs.append(("sV", p.emit("vector", lambda e, s=s: e.tensor_scalar(
                    dst[:, s * D:(s + 1) * D], src[:, s * D:(s + 1) * D],
                    rstd[:, s:s + 1], nm[:, s:s + 1], ALU.mult, ALU.add),
                    inc="sV")))
        return outs

    def ln_emit(src, dst, extra_waits=(), trash=None, sub_waits=None):
        if trash is None:
            trash = XH
        ssq_rdy = ln_stats_emit(src, trash, extra_waits, sub_waits)
        return ln_chain_emit(src, dst, ssq_rdy)

    def transpose_sub(src_bf16, s, sub_wait, affine=False):
        b = tr["pt"]
        tr["pt"] ^= 1
        waits = [sub_wait]
        fw = free.pop(("PT", b), None)
        if fw:
            waits.append(fw)
        v = None
        for c in range(8):
            v = p.emit("tensor", lambda e, s=s, c=c, b=b: e.transpose(
                PT[b][:, c * 128:(c + 1) * 128],
                src_bf16[:, s * D + c * 128:s * D + (c + 1) * 128], ident[:, :]),
                waits=waits if c == 0 else (),
                inc="sT" if c == 7 else None)
        if affine:
            done = None
            for c in range(8):
                done = p.emit("scalar", lambda e, c=c, s=s, b=b: e.activation(
                    XT[:, c * TT + 128 * s: c * TT + 128 * (s + 1)],
                    PT[b][:, c * 128:(c + 1) * 128],
                    AF.Identity, scale=gfm[:, c:c + 1], bias=bvfm[:, c:c + 1]),
                    waits=[("sT", v)] if c == 0 else (),
                    inc="sS" if c == 7 else None)
            free[("PT", b)] = ("sS", done)
            return [("sS", done)]
        dst = XT[:, :].rearrange("q (c t) -> q c t", t=TT)[:, :, 128 * s:128 * (s + 1)]
        srcv = PT[b][:, 0:1024].rearrange("q (c t) -> q c t", t=128)
        done = p.emit("vector", lambda e, d=dst, sv=srcv: e.tensor_copy(d, sv),
                      waits=[("sT", v)], inc="sV")
        free[("PT", b)] = ("sV", done)
        return [("sV", done)]

    def moe_emit(tile, m, xt_subs, res_out, res_in, res_out_waits=(), mid_hooks=None):
        lg_last = None
        for s in range(NSUB):
            waits = list(xt_subs[s])
            if s == 0:
                fw = bank_free_wait(5)
                if fw:
                    waits.append(fw)
            for c in range(8):
                lg_last = p.emit("tensor", lambda e, s=s, c=c: e.matmul(
                    PM[5][:, s * NE:(s + 1) * NE],
                    XT[:, c * TT + 128 * s: c * TT + 128 * (s + 1)],
                    gwT[:, c * NE:(c + 1) * NE],
                    start=(c == 0), stop=(c == 7)),
                    waits=waits if c == 0 else (),
                    inc="sT" if c == 7 else None)
        lgt_rdy = p.emit("vector", lambda e: e.scalar_tensor_tensor(
            lgt[:, :], PM[5][:, 0:NSUB * NE], cst[:, C_NEG1:C_NEG1 + 1],
            gb4[:, :], ALU.bypass, ALU.add),
            waits=[("sT", lg_last)], inc="sV")
        free[("PM", 5)] = ("sV", lgt_rdy)

        m1, m1n, m2 = st_["m1"], st_["m1n"], st_["m2"]
        zz, zz2, zinv = st_["zz"], st_["zz2"], st_["zinv"]
        for s in range(NSUB):
            p.emit("vector", lambda e, s=s: e.tensor_reduce(
                m1[:, s:s + 1], lgt[:, s * NE:(s + 1) * NE], AX.X, ALU.max))
        p.emit("vector", lambda e: e.tensor_copy(dmy[:, :], cst[:, :]))
        p.emit("vector", lambda e: e.tensor_scalar(
            m1n[:, :], m1[:, :], cst[:, C_NEG1:C_NEG1 + 1], None, ALU.mult))
        for s in range(NSUB):
            p.emit("vector", lambda e, s=s: e.tensor_scalar(
                lmask[:, s * NE:(s + 1) * NE], lgt[:, s * NE:(s + 1) * NE],
                m1[:, s:s + 1], None, ALU.is_lt))
        m1n_rdy = p.emit("vector", lambda e: e.drain(), inc="sV")
        for s in range(NSUB):
            p.emit("vector", lambda e, s=s: e.select(
                lp[:, s * NE:(s + 1) * NE], lmask[:, s * NE:(s + 1) * NE],
                lgt[:, s * NE:(s + 1) * NE], negbig[:, s * NE:(s + 1) * NE]))
        for s in range(NSUB):
            p.emit("vector", lambda e, s=s: e.tensor_reduce(
                m2[:, s:s + 1], lp[:, s * NE:(s + 1) * NE], AX.X, ALU.max))
        p.emit("vector", lambda e: e.tensor_copy(dmy[:, :], cst[:, :]))
        for s in range(NSUB):
            p.emit("vector", lambda e, s=s: e.tensor_scalar(
                keep[:, s * NE:(s + 1) * NE], lgt[:, s * NE:(s + 1) * NE],
                m2[:, s:s + 1], None, ALU.is_ge))
        keep_rdy = p.emit("vector", lambda e: e.drain(), inc="sV")
        exps = []
        for s in range(NSUB):
            exps.append(p.emit("scalar", lambda e, s=s: e.activation(
                e8g[:, s * NE:(s + 1) * NE], lgt[:, s * NE:(s + 1) * NE],
                AF.Exp, bias=m1n[:, s:s + 1]),
                waits=[("sV", m1n_rdy)], inc="sS"))
        p.emit("vector", lambda e: e.tensor_tensor(
            ekg[:, :], e8g[:, :], keep[:, :], ALU.mult),
            waits=[("sS", exps[-1]), ("sV", keep_rdy)])
        for s in range(NSUB):
            p.emit("vector", lambda e, s=s: e.tensor_reduce(
                zz[:, s:s + 1], ekg[:, s * NE:(s + 1) * NE], AX.X, ALU.add))
        p.emit("vector", lambda e: e.tensor_copy(dmy[:, :], cst[:, :]))
        p.emit("vector", lambda e: e.tensor_scalar(
            zz2[:, :], zz[:, :], cst[:, C_TWO:C_TWO + 1], None, ALU.mult))
        p.emit("vector", lambda e: e.tensor_copy(dmy[:, :], cst[:, :]))
        p.emit("vector", lambda e: e.reciprocal(zinv[:, :], zz2[:, :]))
        p.emit("vector", lambda e: e.tensor_copy(dmy[:, :], cst[:, :]))
        g8_last = None
        for s in range(NSUB):
            g8_last = p.emit("vector", lambda e, s=s: e.tensor_scalar(
                g8[:, s * NE:(s + 1) * NE], ekg[:, s * NE:(s + 1) * NE],
                zinv[:, s:s + 1], None, ALU.mult), inc="sV")

        def emit_gt_gb():
            p.emit("vector", lambda e: e.tensor_copy(dmy[:, :], cst[:, :]))
            g8b_rdy = p.emit("vector", lambda e: e.tensor_copy(g8b[:, :], g8[:, :]), inc="sV")
            b = tr["pt"]
            tr["pt"] ^= 1
            waits = [("sV", g8b_rdy)]
            fw = free.pop(("PT", b), None)
            if fw:
                waits.append(fw)
            v = None
            for s in range(NSUB):
                v = p.emit("tensor", lambda e, s=s, b=b: e.transpose(
                    PT[b][0:NE, 128 * s:128 * (s + 1)], g8b[:, s * NE:(s + 1) * NE],
                    ident[:, :]),
                    waits=waits if s == 0 else (),
                    inc="sT" if s == NSUB - 1 else None)
            gT_rdy = p.emit("scalar", lambda e, b=b: e.copy(gT[0:NE, :], PT[b][0:NE, 0:TT]),
                            waits=[("sT", v)], inc="sS")
            free[("PT", b)] = ("sS", gT_rdy)
            init_last = None
            for k in range(NSUB * 2):
                si, h = k // 2, k % 2
                bank = k % 2
                waits = [("sS", gT_rdy)] if k == 0 else []
                fw = bank_free_wait(bank)
                if fw:
                    waits.append(fw)
                v2 = p.emit("tensor", lambda e, si=si, h=h, bank=bank: e.matmul(
                    PM[bank][:, 0:512], gT[0:NE, 128 * si:128 * (si + 1)],
                    bB[0:NE, 512 * h:512 * (h + 1)], start=True, stop=True),
                    waits=waits, inc="sT")
                col = si * D + 512 * h
                iwaits = [("sT", v2)]
                if k == 0:
                    iwaits.extend(res_out_waits)
                init_last = p.emit("vector", lambda e, col=col, bank=bank: e.scalar_tensor_tensor(
                    res_out[:, col:col + 512], PM[bank][:, 0:512],
                    cst[:, C_NEG1:C_NEG1 + 1], res_in[:, col:col + 512],
                    ALU.bypass, ALU.add),
                    waits=iwaits, inc="sV")
                free[("PM", bank)] = ("sV", init_last)
            return init_last

        sub_done = {}
        ev = None
        pending = []
        flat = [(e_i, k) for e_i in range(NE) for k in range(NSUB * 2)]
        wq = wbuf = widx = None
        for idx, (e_i, k) in enumerate(flat):
            if k == 0:
                widx = wload_seq.index((tile, m, e_i))
                emit_wloads(widx + 1)
                wq = qW[(tile, m, e_i)]
                wbuf = widx % NWE
            si, h = k // 2, k % 2
            g2 = e_i * 8 + k
            bank = 2 + (g2 % 3)
            waits = list(xt_subs[si])
            if k == 0:
                waits.append(wq)
            fw = bank_free_wait(bank)
            if fw:
                waits.append(fw)
            last_grp = None
            for f in range(8):
                last_grp = p.emit("tensor", lambda e, si=si, h=h, f=f, bank=bank, wbuf=wbuf: e.matmul(
                    PM[bank][:, 0:512],
                    XT[:, f * TT + 128 * si: f * TT + 128 * (si + 1)],
                    We[wbuf][:, f * D + 512 * h: f * D + 512 * (h + 1)],
                    start=(f == 0), stop=(f == 7)),
                    waits=waits if f == 0 else (),
                    inc="sT" if f == 7 else None)
            col = si * D + 512 * h
            gcol = si * NE + e_i
            if idx < 2:
                pending.append((bank, col, gcol, last_grp))
            else:
                if idx == 2:
                    init_last = emit_gt_gb()
                    for (pbank, pcol, pgcol, pgrp) in pending:
                        ev = p.emit("vector", lambda e, col=pcol, bank=pbank, gcol=pgcol: e.scalar_tensor_tensor(
                            res_out[:, col:col + 512], PM[bank][:, 0:512],
                            g8[:, gcol:gcol + 1], res_out[:, col:col + 512],
                            ALU.mult, ALU.add),
                            waits=[("sT", pgrp), ("sV", init_last)], inc="sV")
                        free[("PM", pbank)] = ("sV", ev)
                    pending = []
                ev = p.emit("vector", lambda e, col=col, bank=bank, gcol=gcol: e.scalar_tensor_tensor(
                    res_out[:, col:col + 512], PM[bank][:, 0:512],
                    g8[:, gcol:gcol + 1], res_out[:, col:col + 512],
                    ALU.mult, ALU.add),
                    waits=[("sT", last_grp)], inc="sV")
                free[("PM", bank)] = ("sV", ev)
            if mid_hooks and idx in mid_hooks:
                mid_hooks[idx]()
            if k == NSUB * 2 - 1:
                wfree.setdefault(widx, [])
                wfree[widx].append(("sT", last_grp))
            if e_i == NE - 1 and h == 1:
                sub_done[si] = ev
        done = p.emit("vector", lambda e: e.drain(), waits=[("sV", ev)], inc="sV")
        return done, sub_done

    def attn_emit(tile, xb, ob_other, r1_done, sc32_wait, r1_subs=None):
        sw = None if r1_subs is None else [("sV", r1_subs[s]) for s in range(NSUB)]
        ew = [("sV", r1_done)] if r1_subs is None else []
        ssq_rdy = ln_stats_emit(R1, XH, extra_waits=ew, sub_waits=sw)
        rstd_rdy, _ = ln_var_emit(ssq_rdy)
        var, rstd, mu = st_["var"], st_["rstd"], st_["mu"]
        musq = st_["musq"]
        tmpv, lnt, r2c = st_["tmpv"], st_["lnt"], st_["r2c"]
        zz2 = st_["zz2"]
        p.emit("vector", lambda e: e.tensor_tensor(musq[:, :], var[:, :], rstd[:, :], ALU.mult),
               waits=[("sS", rstd_rdy)])
        p.emit("vector", lambda e: e.tensor_copy(dmy[:, :], cst[:, :]))
        tv = p.emit("vector", lambda e: e.tensor_tensor(tmpv[:, :], musq[:, :], rstd[:, :], ALU.mult), inc="sV")
        p.emit("scalar", lambda e: e.activation(lnt[:, :], tmpv[:, :], AF.Ln,
                                                bias=np.float32(1e-5)),
               waits=[("sV", tv)])
        p.emit("scalar", lambda e: e.activation(
            r2c[:, :], lnt[:, :], AF.Exp, scale=np.float32(-0.5)))
        iv_rdy = p.emit("scalar", lambda e: e.activation(
            st_["zinv"][:, :], lnt[:, :], AF.Exp, scale=np.float32(0.5)), inc="sS")
        p.emit("vector", lambda e: e.tensor_tensor(zz2[:, :], rstd[:, :], r2c[:, :], ALU.mult),
               waits=[("sS", iv_rdy)])
        p.emit("vector", lambda e: e.tensor_copy(dmy[:, :], cst[:, :]))
        nm2_rdy = p.emit("vector", lambda e: e.scalar_tensor_tensor(
            st_["nm"][:, :], mu[:, :], cst[:, C_NEG1:C_NEG1 + 1], zz2[:, :],
            ALU.mult, ALU.mult), inc="sV")
        subs = []
        for s in range(NSUB):
            subs.append(p.emit("scalar", lambda e, s=s: e.activation(
                XH[:, s * D:(s + 1) * D], R1[:, s * D:(s + 1) * D],
                AF.Identity, scale=zz2[:, s:s + 1], bias=st_["nm"][:, s:s + 1]),
                waits=[("sV", nm2_rdy)], inc="sS"))
        inv2 = st_["zinv"]
        for s in range(NSUB):
            p.emit("vector", lambda e, s=s: e.scalar_tensor_tensor(
                R1[:, s * D:(s + 1) * D], XH[:, s * D:(s + 1) * D],
                inv2[:, s:s + 1], R1[:, s * D:(s + 1) * D],
                ALU.mult, ALU.add),
                waits=[("sS", subs[s])], inc="sV")
        xt_rdy = None
        for s in range(NSUB):
            xt_rdy = transpose_sub(XH, s, ("sS", subs[s]), affine=True)
        if dbg == "B0" and tile == 0:
            fold_done = p.emit("vector", lambda e: e.drain(), inc="sV")
            dma("sync", d_r1, R1[:, :], "sO0", waits=[("sV", fold_done)])
            cv1 = p.emit("vector", lambda e: e.tensor_copy(dbg_f32[:, 0:NSUB * D], XH[:, :]),
                         waits=[("sV", a_sub[-1])], inc="sV")
            dma("sync", d_xh, dbg_f32[:, 0:NSUB * D], "sO0", waits=[("sV", cv1)])
            cv2 = p.emit("vector", lambda e: e.tensor_copy(dbg_f32[:, :], XT[:, :]),
                         waits=[("sO0", p.cnt["sO0"]), ("sV", cv1)], inc="sV")
            dma("sync", d_xt, dbg_f32[:, :], "sO0", waits=[("sV", cv2)])
            dma("sync", d_sc32, SC32[:, :], "sO0", waits=[("sV", cv2)])
            dma("sync", d_sc32b, SC32B[:, :], "sO0", waits=[("sV", cv2)])
            return None

        relu_cnt, sq_cnt, stage2_done = {}, {}, {}
        out_evac = {}

        def emit_stage2(h, g, k):
            if g in stage2_done:
                return
            jj, ic = k // 8, k % 8
            v = None
            for s in range(NSUB):
                waits = []
                if s == 0:
                    waits.append(("sV", sq_cnt[g]))
                if k == 0:
                    fw = bank_free_wait(s)
                    if fw:
                        waits.append(fw)
                v = p.emit("tensor", lambda e, s=s, jj=jj, ic=ic, g=g, k=k: e.matmul(
                    PM[s][:, 0:256],
                    Pp[g % 3][:, 128 * s:128 * (s + 1)],
                    wprojT[:, ic * D + 256 * jj: ic * D + 256 * (jj + 1)],
                    start=(k == 0), stop=(k == 31)),
                    waits=waits, inc="sT" if s == NSUB - 1 else None)
            stage2_done[g] = v

        g_global = 0
        nh_run = 1 if dbg in ("B3", "B4", "B5") else NH
        nj_run = 1 if dbg in ("B4", "B5") else 4
        ni_run = 1 if dbg == "B4" else 8
        for h in range(nh_run):
            for jj in range(nj_run):
                for ic in range(ni_run):
                    g = g_global
                    g_global += 1
                    k = jj * 8 + ic
                    sbank = 4 + (g % 2)
                    waits = []
                    if g == 0:
                        waits.extend(xt_rdy)
                    if g >= 2:
                        waits.append(("sS", relu_cnt[g - 2]))
                    if g < 2:
                        fw = bank_free_wait(sbank)
                        if fw:
                            waits.append(fw)
                    v = None
                    for dc in range(2):
                        cc_w = 2 * jj + dc
                        cc_a = 2 * h + dc
                        v = p.emit("tensor", lambda e, cc_w=cc_w, cc_a=cc_a, ic=ic, sbank=sbank, dc=dc: e.matmul(
                            PM[sbank][:, 0:512],
                            wfcT[:, cc_w * D + 128 * ic: cc_w * D + 128 * (ic + 1)],
                            XT[:, cc_a * TT:(cc_a + 1) * TT],
                            start=(dc == 0), stop=(dc == 1)),
                            waits=waits if dc == 0 else (),
                            inc="sT" if dc == 1 else None)
                    rwaits = [("sT", v)]
                    if g >= 2:
                        rwaits.append(("sV", sq_cnt[g - 2]))
                    relu_cnt[g] = p.emit("scalar", lambda e, g=g, sbank=sbank: e.activation(
                        RL[g % 2][:, :], PM[sbank][:, 0:512], AF.Relu),
                        waits=rwaits, inc="sS")
                    qwaits = [("sS", relu_cnt[g])]
                    if g >= 3:
                        qwaits.append(("sT", stage2_done[g - 3]))
                    sq_cnt[g] = p.emit("vector", lambda e, g=g: e.tensor_tensor(
                        Pp[g % 3][:, :], RL[g % 2][:, :], RL[g % 2][:, :], ALU.mult),
                        waits=qwaits, inc="sV")
                    if dbg == "B2" and tile == 0 and g == 0:
                        cv = p.emit("vector", lambda e: e.tensor_copy(
                            dbg_f32[:, 0:512], PM[4][:, 0:512]),
                            waits=[("sS", relu_cnt[0])], inc="sV")
                        cv2 = p.emit("vector", lambda e: e.tensor_copy(
                            dbg_f32[:, 512:1024], RL[0][:, :]), inc="sV")
                        cv3 = p.emit("vector", lambda e: e.tensor_copy(
                            dbg_f32[:, 1024:1536], Pp[0][:, :]),
                            waits=[("sV", sq_cnt[0])], inc="sV")
                        cv4 = p.emit("vector", lambda e: e.tensor_copy(
                            dbg_f32[:, 1536:2048], wfcT[:, 0:512]), inc="sV")
                        cv5 = p.emit("vector", lambda e: e.tensor_copy(
                            dbg_f32[:, 2048:2560], XT[:, 0:512]), inc="sV")
                        cv6 = p.emit("vector", lambda e: e.tensor_copy(
                            dbg_f32[:, 2560:3072], XT[:, 512:1024]),
                            waits=[("sV", cv3)], inc="sV")
                        dma("sync", d_xt, dbg_f32[:, :], "sO0", waits=[("sV", cv6)])
                        return None
                    if k >= 2:
                        emit_stage2(h, g - 2, k - 2)
            if dbg == "B4":
                emit_stage2(h, 0, 0)
            elif dbg == "B5":
                emit_stage2(h, g_global - 2, 6)
                emit_stage2(h, g_global - 1, 7)
            else:
                emit_stage2(h, g_global - 2, 30)
                emit_stage2(h, g_global - 1, 31)
            for s in range(NSUB):
                evv = p.emit("vector", lambda e, s=s, h=h: e.scalar_tensor_tensor(
                    R1[:, s * D + 256 * h: s * D + 256 * (h + 1)],
                    PM[s][:, 0:256], cst[:, C_NEG1:C_NEG1 + 1],
                    R1[:, s * D + 256 * h: s * D + 256 * (h + 1)],
                    ALU.bypass, ALU.add),
                    waits=[("sT", stage2_done[g_global - 1])], inc="sV")
                out_evac[(h, s)] = evv
                free[("PM", s)] = ("sV", evv)
        if g_global >= 2:
            free[("PM", 4)] = ("sS", relu_cnt[g_global - 2])
            free[("PM", 5)] = ("sS", relu_cnt[g_global - 1])
        x3_subs = {s: out_evac[(nh_run - 1, s)] for s in range(NSUB)}
        return p.emit("vector", lambda e: e.drain(),
                      waits=[("sV", out_evac[(nh_run - 1, NSUB - 1)])], inc="sV"), x3_subs

    x_scratch_done = {}
    out_free = {}
    subs_next = None
    for i in range(n_tiles):
        xb = i % 2
        ob = 0
        if i + 1 < n_tiles:
            waits = []
            if (i - 1) in x_scratch_done:
                waits.append(x_scratch_done[i - 1])
            sem = f"sQx{(i + 1) % 2}"
            qX.append((sem, dma("sync", X[(i + 1) % 2][:, :],
                                x_h[TT * (i + 1):TT * (i + 2), :].rearrange("(s q) f -> q s f", q=128),
                                sem, waits=waits)))
        if subs_next is None:
            subs = ln_emit(X[xb], XH1, extra_waits=[qX[i]], trash=XH1)
        else:
            subs = subs_next
        xt_subs = [transpose_sub(XH1, s, subs[s]) for s in range(NSUB)]
        r1_done, r1_subs = moe_emit(i, 0, xt_subs, R1, X[xb])
        if dbg == "A" and i == 0:
            cv1 = p.emit("vector", lambda e: e.tensor_copy(dbg_f32[:, 0:NSUB * D], XH[:, :]),
                         waits=[("sV", r1_done)], inc="sV")
            dma("sync", d_xh, dbg_f32[:, 0:NSUB * D], "sO0", waits=[("sV", cv1)])
            cv2 = p.emit("vector", lambda e: e.tensor_copy(dbg_f32[:, :], XT[:, :]),
                         waits=[("sO0", p.cnt["sO0"]), ("sV", cv1)], inc="sV")
            dma("sync", d_xt, dbg_f32[:, :], "sO0", waits=[("sV", cv2)])
            dma("sync", d_lgt, lgt[:, :], "sO0", waits=[("sV", r1_done)])
            dma("sync", d_g8, g8[:, :], "sO0", waits=[("sV", r1_done)])
            dma("sync", d_r1, R1[:, :], "sO0", waits=[("sV", r1_done)])
            break
        x3_res = attn_emit(i, xb, 1 - ob, r1_done, out_free.get(1 - ob),
                           r1_subs=r1_subs)
        if x3_res is None:
            break
        x3_done, x3_subs = x3_res
        if dbg in ("B3", "B4", "B5") and i == 0:
            dma("sync", d_r1, R1[:, :], "sO0", waits=[("sV", x3_done)])
            if dbg in ("B4", "B5"):
                cvp = p.emit("vector", lambda e: e.tensor_copy(
                    dbg_f32[:, 0:512], Pp[0][:, :]), waits=[("sV", x3_done)], inc="sV")
                cvr = p.emit("vector", lambda e: e.tensor_copy(
                    dbg_f32[:, 512:1024], RL[0][:, :]), inc="sV")
                cvs = p.emit("vector", lambda e: e.tensor_copy(
                    dbg_f32[:, 1024:1536], PM[4][:, 0:512]), inc="sV")
                cvs = p.emit("vector", lambda e: e.tensor_copy(
                    dbg_f32[:, 1536:2048], PM[5][:, 0:512]), inc="sV")
                dma("sync", d_xt, dbg_f32[:, :], "sO0", waits=[("sV", cvs)])
            break
        if dbg == "C0" and i == 0:
            dma("sync", d_r1, R1[:, :], "sO0", waits=[("sV", x3_done)])
        if dbg == "B0" and i == 0:
            break
        x_scratch_done[i] = ("sV", x3_done)
        if dbg == "B" and i == 0:
            dma("sync", d_r1, R1[:, :], "sO0", waits=[("sV", x3_done)])
            break
        if dbg == "C4" and i == 0:
            for _ in range(8):
                p.emit("vector", lambda e: e.tensor_copy(dmy[:, :], cst[:, :]))
                p.emit("scalar", lambda e: e.activation(dmy[:, :], cst[:, :], AF.Identity))
            dd = p.emit("vector", lambda e: e.drain(), waits=[("sV", x3_done)], inc="sV")
            dma("sync", d_r1, R1[:, :], "sO0", waits=[("sV", dd)])
            break
        subs3 = ln_emit(R1, XH, sub_waits=[("sV", x3_subs[s]) for s in range(NSUB)])
        if dbg == "C1" and i == 0:
            dma("sync", d_r1, R1[:, :], "sO0", waits=[("sS", subs3[-1])])
            break
        xt3 = [transpose_sub(XH, s, subs3[s]) for s in range(NSUB)]
        if dbg == "C2" and i == 0:
            dma("sync", d_r1, R1[:, :], "sO0", waits=list(xt3[-1]))
            break
        ow = [out_free[ob]] if ob in out_free else []
        holder = {}

        def ln1_stats(i=i, holder=holder):
            if i + 1 < n_tiles and dbg is None:
                holder["st"] = ln_stats_emit(X[(i + 1) % 2], XH1,
                                             extra_waits=[qX[i + 1]])

        def ln1_rest(i=i, holder=holder):
            if i + 1 < n_tiles and dbg is None:
                holder["subs"] = ln_chain_emit(X[(i + 1) % 2], XH1, holder["st"])

        out_done, _ = moe_emit(i, 1, xt3, OUT[ob], R1, res_out_waits=ow,
                               mid_hooks={3: ln1_stats, 15: ln1_rest})
        subs_next = holder.get("subs")
        if dbg == "C" and i == 0:
            cv1 = p.emit("vector", lambda e: e.tensor_copy(dbg_f32[:, 0:NSUB * D], XH[:, :]),
                         waits=[("sV", out_done)], inc="sV")
            dma("sync", d_xh, dbg_f32[:, 0:NSUB * D], "sO0", waits=[("sV", cv1)])
            cv2 = p.emit("vector", lambda e: e.tensor_copy(dbg_f32[:, :], XT[:, :]),
                         waits=[("sO0", p.cnt["sO0"]), ("sV", cv1)], inc="sV")
            dma("sync", d_xt, dbg_f32[:, :], "sO0", waits=[("sV", cv2)])
            dma("sync", d_lgt, lgt[:, :], "sO0", waits=[("sV", out_done)])
            dma("sync", d_g8, g8[:, :], "sO0", waits=[("sV", out_done)])
            dma("sync", d_r1, R1[:, :], "sO0", waits=[("sV", out_done)])
            dma("sync", d_sc32, OUT[ob][:, :], "sO0", waits=[("sV", out_done)])
            break
        dst = out_h[TT * i:TT * (i + 1), :].rearrange("(s q) f -> q s f", q=128)
        sem = f"sO{ob}"
        sq_ = dma("sync", dst, OUT[ob][:, :].rearrange("q (s f) -> q s f", f=D),
                  sem, waits=[("sV", out_done)])
        out_free[ob] = (sem, sq_)

    final_stores = [("sO0", p.cnt["sO0"]), ("sO1", p.cnt["sO1"])]

    sems = {s: es.enter_context(nc.semaphore(s))
            for s in ["sT", "sV", "sS", "sQc", "sQx0", "sQx1",
                      "sGs", "sGw0", "sGw1", "sGw2", "sO0", "sO1", "sB0", "sB1"]}
    block = es.enter_context(nc.Block())

    def mk(name):
        def f(eng):
            if name in ("tensor", "vector", "scalar"):
                eng.wait_ge(sems["sQc"], q_consts)
                eng.wait_ge(sems["sGs"], g_static)
            p.replay(name, eng, sems)
            if name == "sync":
                for s_, v_ in final_stores:
                    if v_:
                        eng.wait_ge(sems[s_], v_)
        return f

    block.sync(mk("sync"))
    block.gpsimd(mk("gpsimd"))
    block.vector(mk("vector"))
    block.scalar(mk("scalar"))
    block.tensor(mk("tensor"))

    es.close()
    return nc


def make_common(w_fc, w_proj, ln_g, ln_b, gate_w, gate_b, expert_w, expert_b):
    return {
        "wfcT": np.ascontiguousarray(np.asarray(w_fc, dtype=np.float32).T),
        "wprojT": np.ascontiguousarray(np.asarray(w_proj, dtype=np.float32).T),
        "gwT": np.ascontiguousarray(np.asarray(gate_w, dtype=np.float32).T),
        "ew": np.ascontiguousarray(np.asarray(expert_w, dtype=np.float32)),
        "eb": np.ascontiguousarray(np.asarray(expert_b, dtype=np.float32)),
        "gfm": np.ascontiguousarray(np.asarray(ln_g, dtype=np.float32).reshape(8, 128).T),
        "bvfm": np.ascontiguousarray(np.asarray(ln_b, dtype=np.float32).reshape(8, 128).T),
        "gb4": np.ascontiguousarray(np.tile(
            np.asarray(gate_b, dtype=np.float32), (128, NSUB))),
        "cst": np.ascontiguousarray(np.tile(np.array(
            [1.0 / D, -1.0, 2.0, 0, 0, 0, 0, 0], dtype=np.float32), (128, 1))),
        "ident": np.eye(128, dtype=ml_dtypes.bfloat16),
        "ident32": np.eye(128, dtype=np.float32),
    }


def kernel(L0_attn_residual, L0_attn_w_fc, L0_attn_w_proj, L0_attn_ln_g,
           L0_attn_ln_b, L0_ffn_gate_w, L0_ffn_gate_b, L0_ffn_expert_w,
           L0_ffn_expert_b):
    xfull = np.ascontiguousarray(
        np.asarray(L0_attn_residual, dtype=np.float32).reshape(-1, D))
    per = xfull.shape[0] // NCORE
    n_tiles = per // TT
    common = make_common(L0_attn_w_fc, L0_attn_w_proj, L0_attn_ln_g,
                         L0_attn_ln_b, L0_ffn_gate_w, L0_ffn_gate_b,
                         L0_ffn_expert_w, L0_ffn_expert_b)
    nc = build_program(n_tiles)
    in_maps = [dict(common, x=np.ascontiguousarray(xfull[c * per:(c + 1) * per]))
               for c in range(NCORE)]
    res = run_bass_kernel_spmd(nc, in_maps, core_ids=list(range(NCORE)))
    out = np.concatenate([res.results[c]["out"] for c in range(NCORE)], axis=0)
    return out.reshape(np.asarray(L0_attn_residual).shape).astype(np.float32)


# version 19
# speedup vs baseline: 1.0694x; 1.0402x over previous
import numpy as np
import ml_dtypes
import concourse.bass as bass
import concourse.mybir as mybir
from concourse.bass_utils import run_bass_kernel_spmd

F32 = mybir.dt.float32
BF16 = mybir.dt.bfloat16
U8 = mybir.dt.uint8
AF = mybir.ActivationFunctionType
ALU = mybir.AluOpType
AX = mybir.AxisListType

D = 1024
NSUB = 4
TT = 128 * NSUB
NE = 8
NH = 4
NCORE = 8


class Plan:

    ENGS = ("tensor", "vector", "scalar", "sync", "gpsimd")

    def __init__(self):
        self.ops = {e: [] for e in self.ENGS}
        self.cnt = {s: 0 for s in ("sT", "sV", "sS", "sQc", "sQx0", "sQx1",
                                    "sGs", "sGw0", "sGw1", "sGw2", "sO0", "sO1",
                                    "sB0", "sB1")}

    def emit(self, eng, fn, waits=(), inc=None, inc_by=1):
        for s, v in waits:
            assert v <= self.cnt[s], f"wait {s}>={v} but only {self.cnt[s]} emitted"
        if inc is not None:
            self.cnt[inc] += inc_by
        self.ops[eng].append((fn, tuple(waits), inc, inc_by))
        return self.cnt[inc] if inc is not None else None

    def replay(self, eng_name, eng, sems):
        last = {}
        for fn, waits, inc, inc_by in self.ops[eng_name]:
            for s, v in waits:
                if last.get(s, -1) < v:
                    eng.wait_ge(sems[s], v)
                    last[s] = v
            inst = fn(eng)
            if inc is not None:
                inst.then_inc(sems[inc], inc_by)


def build_program(n_tiles, dbg=None):
    nc = bass.Bass("TRN2", target_bir_lowering=False, detect_race_conditions=False)
    ntok = TT * n_tiles

    x_h = nc.dram_tensor("x", [ntok, D], F32, kind="ExternalInput").ap()
    wfcT_h = nc.dram_tensor("wfcT", [D, D], F32, kind="ExternalInput").ap()
    wprojT_h = nc.dram_tensor("wprojT", [D, D], F32, kind="ExternalInput").ap()
    gwT_h = nc.dram_tensor("gwT", [D, NE], F32, kind="ExternalInput").ap()
    ew_h = nc.dram_tensor("ew", [NE, D, D], F32, kind="ExternalInput").ap()
    eb_h = nc.dram_tensor("eb", [NE, D], F32, kind="ExternalInput").ap()
    gfm_h = nc.dram_tensor("gfm", [128, 8], F32, kind="ExternalInput").ap()
    bvfm_h = nc.dram_tensor("bvfm", [128, 8], F32, kind="ExternalInput").ap()
    gb4_h = nc.dram_tensor("gb4", [128, NSUB * NE], F32, kind="ExternalInput").ap()
    cst_h = nc.dram_tensor("cst", [128, 8], F32, kind="ExternalInput").ap()
    ident_h = nc.dram_tensor("ident", [128, 128], BF16, kind="ExternalInput").ap()
    ident32_h = nc.dram_tensor("ident32", [128, 128], F32, kind="ExternalInput").ap()
    ewb_h = nc.dram_tensor("ewb", [NE, D, D], BF16, kind="Internal").ap()
    out_h = nc.dram_tensor("out", [ntok, D], F32, kind="ExternalOutput").ap()
    if dbg:
        d_xh = nc.dram_tensor("d_xh", [128, NSUB * D], F32, kind="ExternalOutput").ap()
        d_xt = nc.dram_tensor("d_xt", [128, 8 * TT], F32, kind="ExternalOutput").ap()
        d_lgt = nc.dram_tensor("d_lgt", [128, NSUB * NE], F32, kind="ExternalOutput").ap()
        d_g8 = nc.dram_tensor("d_g8", [128, NSUB * NE], F32, kind="ExternalOutput").ap()
        d_r1 = nc.dram_tensor("d_r1", [128, NSUB * D], F32, kind="ExternalOutput").ap()
        d_sc32 = nc.dram_tensor("d_sc32", [128, NSUB * D], F32, kind="ExternalOutput").ap()
        d_sc32b = nc.dram_tensor("d_sc32b", [128, NSUB * D], F32, kind="ExternalOutput").ap()


    from contextlib import ExitStack
    es = ExitStack()
    sbt = lambda n, sh, dt=F32: es.enter_context(nc.sbuf_tensor(n, sh, dt))
    pst = lambda n, sh, dt=F32: es.enter_context(nc.psum_tensor(n, sh, dt))

    wfcT = sbt("wfcT_s", [128, 8 * D], BF16)
    wprojT = sbt("wprojT_s", [128, 8 * D], BF16)
    gwT = sbt("gwT_s", [128, 8 * NE], BF16)
    bB = sbt("bB_s", [NE, D], BF16)
    gfm = sbt("gfm_s", [128, 8])
    bvfm = sbt("bvfm_s", [128, 8])
    gb4 = sbt("gb4_s", [128, NSUB * NE])
    cst = sbt("cst_s", [128, 8])
    ident = sbt("ident_s", [128, 128], BF16)
    ident32 = sbt("ident32_s", [128, 128])
    negbig = sbt("negbig_s", [128, NSUB * NE])
    dmy = sbt("dmy_s", [128, 8])

    NWE = 3
    We = [sbt(f"We{b}", [128, 8 * D], BF16) for b in range(NWE)]
    X = [sbt(f"X{b}", [128, NSUB * D]) for b in range(2)]
    R1 = sbt("R1", [128, NSUB * D])
    OUT = [sbt("OUT0", [128, NSUB * D])]
    XH = sbt("XH", [128, NSUB * D], BF16)
    XH1 = sbt("XH1", [128, NSUB * D], BF16)
    XT = sbt("XT", [128, 8 * TT], BF16)
    RL = [sbt(f"RL{b}", [128, 512], BF16) for b in range(2)]
    Pp = [sbt(f"Pp{b}", [128, 512], BF16) for b in range(3)]
    gT = sbt("gT_s", [NE, TT], BF16)
    g8 = sbt("g8_s", [128, NSUB * NE])
    g8b = sbt("g8b_s", [128, NSUB * NE], BF16)

    lgt = sbt("lgt", [128, NSUB * NE])
    lmask = sbt("lmask", [128, NSUB * NE], U8)
    lp = sbt("lp", [128, NSUB * NE])
    keep = sbt("keep", [128, NSUB * NE])
    e8g = sbt("e8g", [128, NSUB * NE])
    ekg = sbt("ekg", [128, NSUB * NE])
    st_ = {n: sbt(n, [128, NSUB]) for n in
           ["ssum", "ssq", "mu", "musq", "var", "lnv", "rstd", "nm",
            "m1", "m1n", "m2", "zz", "zz2", "zinv", "r2c", "tmpv", "lnt"]}

    dbg_f32 = sbt("dbg_f32", [128, 8 * TT]) if dbg else None
    PM = [pst(f"PM{i}", [128, 512]) for i in range(6)]
    PT = [pst(f"PTb{i}", [128, 1024], BF16) for i in range(2)]

    C_1024INV, C_NEG1, C_TWO = 0, 1, 2

    p = Plan()
    free = {}
    tr = {"pt": 0}

    def bank_free_wait(bank):
        return free.pop(("PM", bank), None)

    def dma(eng, dst, src, sem, waits=()):
        return p.emit(eng, lambda e, d=dst, s=src: e.dma_start(d, s),
                      waits=waits, inc=sem, inc_by=16)

    dma("sync", cst[:, :], cst_h, "sQc")
    dma("sync", gfm[:, :], gfm_h, "sQc")
    dma("sync", bvfm[:, :], bvfm_h, "sQc")
    dma("sync", gb4[:, :], gb4_h, "sQc")
    dma("sync", ident[:, :], ident_h, "sQc")
    dma("sync", ident32[:, :], ident32_h, "sQc")
    q_consts = p.cnt["sQc"]
    qX = [("sQx0", dma("sync", X[0][:, :],
                       x_h[0:TT, :].rearrange("(s q) f -> q s f", q=128), "sQx0"))]

    dma("gpsimd", wfcT[:, :], wfcT_h.rearrange("(c q) i -> q c i", q=128), "sGs")
    dma("gpsimd", wprojT[:, :], wprojT_h.rearrange("(c q) i -> q c i", q=128), "sGs")
    dma("gpsimd", gwT[:, :], gwT_h.rearrange("(c q) e -> q c e", q=128), "sGs")
    g_static = dma("gpsimd", bB[:, :], eb_h, "sGs")

    p.emit("vector", lambda e: e.memset(negbig[:, :], -1e30))

    wload_seq = [(i, m, e) for i in range(n_tiles) for m in range(2) for e in range(NE)]
    qW, wfree = {}, {}
    wstate = {"emitted": 0}

    bstore_totals = {}

    def emit_wloads(upto):
        while wstate["emitted"] < upto:
            k = wstate["emitted"]
            tile, m, e = wload_seq[k]
            waits = []
            if k - NWE in wfree:
                waits.extend(wfree[k - NWE])
            sem = f"sGw{k % NWE}"
            if k < NE:
                q = dma("gpsimd", We[k % NWE][:, :],
                        ew_h[e].rearrange("(c q) o -> q c o", q=128), sem, waits=waits)
                bsem = f"sB{k % 2}"
                bq = dma("sync", ewb_h[e].rearrange("(c q) o -> q c o", q=128),
                         We[k % NWE][:, :], bsem, waits=[(sem, q)])
                bstore_totals[bsem] = bq
                wfree.setdefault(k, [])
                wfree[k].append((bsem, bq))
                qW[(tile, m, e)] = (sem, q)
            else:
                waits2 = list(waits)
                if k == NE:
                    for bs, bv in bstore_totals.items():
                        waits2.append((bs, bv))
                qW[(tile, m, e)] = (sem, dma(
                    "gpsimd", We[k % NWE][:, :],
                    ewb_h[e].rearrange("(c q) o -> q c o", q=128), sem, waits=waits2))
            wstate["emitted"] += 1

    def ln_stats_emit(src, trash, extra_waits=(), sub_waits=None):
        ssum, ssq = st_["ssum"], st_["ssq"]
        ew = tuple(extra_waits)

        def wsub(s):
            w = list(ew) if s == 0 else []
            if sub_waits is not None:
                w.append(sub_waits[s])
            return w

        for s in range(NSUB):
            p.emit("vector", lambda e, s=s: e.tensor_reduce(
                ssum[:, s:s + 1], src[:, s * D:(s + 1) * D], AX.X, ALU.add),
                waits=wsub(s))
        for s in range(NSUB):
            p.emit("scalar", lambda e, s=s: e.activation(
                trash[:, s * D:(s + 1) * D], src[:, s * D:(s + 1) * D],
                AF.Square, accum_out=ssq[:, s:s + 1]),
                waits=wsub(s))
        p.emit("scalar", lambda e: e.activation(dmy[:, :], cst[:, :], AF.Identity))
        return p.emit("scalar", lambda e: e.drain(), inc="sS")

    def ln_var_emit(ssq_rdy):
        ssum, ssq = st_["ssum"], st_["ssq"]
        mu, musq, var = st_["mu"], st_["musq"], st_["var"]
        lnv, rstd, nm = st_["lnv"], st_["rstd"], st_["nm"]
        p.emit("vector", lambda e: e.tensor_copy(dmy[:, :], cst[:, :]))
        p.emit("vector", lambda e: e.tensor_scalar(
            mu[:, :], ssum[:, :], cst[:, C_1024INV:C_1024INV + 1], None, ALU.mult))
        p.emit("vector", lambda e: e.tensor_copy(dmy[:, :], cst[:, :]))
        p.emit("vector", lambda e: e.tensor_tensor(musq[:, :], mu[:, :], mu[:, :], ALU.mult))
        p.emit("vector", lambda e: e.tensor_copy(dmy[:, :], cst[:, :]))
        p.emit("vector", lambda e: e.scalar_tensor_tensor(
            var[:, :], ssq[:, :], cst[:, C_1024INV:C_1024INV + 1], musq[:, :],
            ALU.mult, ALU.subtract), waits=[("sS", ssq_rdy)])
        var_rdy = p.emit("vector", lambda e: e.drain(), inc="sV")
        p.emit("scalar", lambda e: e.activation(lnv[:, :], var[:, :], AF.Ln,
                                                bias=np.float32(1e-5)),
               waits=[("sV", var_rdy)])
        rstd_rdy = p.emit("scalar", lambda e: e.activation(
            rstd[:, :], lnv[:, :], AF.Exp, scale=np.float32(-0.5)), inc="sS")
        nm_rdy = p.emit("vector", lambda e: e.scalar_tensor_tensor(
            nm[:, :], mu[:, :], cst[:, C_NEG1:C_NEG1 + 1], rstd[:, :],
            ALU.mult, ALU.mult), waits=[("sS", rstd_rdy)], inc="sV")
        return rstd_rdy, nm_rdy

    def ln_chain_emit(src, dst, ssq_rdy):
        rstd, nm = st_["rstd"], st_["nm"]
        _, nm_rdy = ln_var_emit(ssq_rdy)
        p.emit("vector", lambda e: e.tensor_copy(dmy[:, :], cst[:, :]))
        p.emit("vector", lambda e: e.tensor_copy(dmy[:, :], cst[:, :]))
        outs = []
        for s in range(NSUB):
            if s % 2 == 0:
                outs.append(("sS", p.emit("scalar", lambda e, s=s: e.activation(
                    dst[:, s * D:(s + 1) * D], src[:, s * D:(s + 1) * D],
                    AF.Identity, scale=rstd[:, s:s + 1], bias=nm[:, s:s + 1]),
                    waits=[("sV", nm_rdy)], inc="sS")))
            else:
                outs.append(("sV", p.emit("vector", lambda e, s=s: e.tensor_scalar(
                    dst[:, s * D:(s + 1) * D], src[:, s * D:(s + 1) * D],
                    rstd[:, s:s + 1], nm[:, s:s + 1], ALU.mult, ALU.add),
                    inc="sV")))
        return outs

    def ln_emit(src, dst, extra_waits=(), trash=None, sub_waits=None):
        if trash is None:
            trash = XH
        ssq_rdy = ln_stats_emit(src, trash, extra_waits, sub_waits)
        return ln_chain_emit(src, dst, ssq_rdy)

    def transpose_sub(src_bf16, s, sub_wait, affine=False):
        b = tr["pt"]
        tr["pt"] ^= 1
        waits = [sub_wait]
        fw = free.pop(("PT", b), None)
        if fw:
            waits.append(fw)
        v = None
        for c in range(8):
            v = p.emit("tensor", lambda e, s=s, c=c, b=b: e.transpose(
                PT[b][:, c * 128:(c + 1) * 128],
                src_bf16[:, s * D + c * 128:s * D + (c + 1) * 128], ident[:, :]),
                waits=waits if c == 0 else (),
                inc="sT" if c == 7 else None)
        if affine:
            done = None
            for c in range(8):
                done = p.emit("scalar", lambda e, c=c, s=s, b=b: e.activation(
                    XT[:, c * TT + 128 * s: c * TT + 128 * (s + 1)],
                    PT[b][:, c * 128:(c + 1) * 128],
                    AF.Identity, scale=gfm[:, c:c + 1], bias=bvfm[:, c:c + 1]),
                    waits=[("sT", v)] if c == 0 else (),
                    inc="sS" if c == 7 else None)
            free[("PT", b)] = ("sS", done)
            return [("sS", done)]
        dst = XT[:, :].rearrange("q (c t) -> q c t", t=TT)[:, :, 128 * s:128 * (s + 1)]
        srcv = PT[b][:, 0:1024].rearrange("q (c t) -> q c t", t=128)
        done = p.emit("vector", lambda e, d=dst, sv=srcv: e.tensor_copy(d, sv),
                      waits=[("sT", v)], inc="sV")
        free[("PT", b)] = ("sV", done)
        return [("sV", done)]

    def moe_emit(tile, m, xt_subs, res_out, res_in, res_out_waits=(), mid_hooks=None):
        lg_last = None
        for s in range(NSUB):
            waits = list(xt_subs[s])
            if s == 0:
                fw = bank_free_wait(5)
                if fw:
                    waits.append(fw)
            for c in range(8):
                lg_last = p.emit("tensor", lambda e, s=s, c=c: e.matmul(
                    PM[5][:, s * NE:(s + 1) * NE],
                    XT[:, c * TT + 128 * s: c * TT + 128 * (s + 1)],
                    gwT[:, c * NE:(c + 1) * NE],
                    start=(c == 0), stop=(c == 7)),
                    waits=waits if c == 0 else (),
                    inc="sT" if c == 7 else None)
        lgt_rdy = p.emit("vector", lambda e: e.scalar_tensor_tensor(
            lgt[:, :], PM[5][:, 0:NSUB * NE], cst[:, C_NEG1:C_NEG1 + 1],
            gb4[:, :], ALU.bypass, ALU.add),
            waits=[("sT", lg_last)], inc="sV")
        free[("PM", 5)] = ("sV", lgt_rdy)

        m1, m1n, m2 = st_["m1"], st_["m1n"], st_["m2"]
        zz, zz2, zinv = st_["zz"], st_["zz2"], st_["zinv"]
        for s in range(NSUB):
            p.emit("vector", lambda e, s=s: e.tensor_reduce(
                m1[:, s:s + 1], lgt[:, s * NE:(s + 1) * NE], AX.X, ALU.max))
        p.emit("vector", lambda e: e.tensor_copy(dmy[:, :], cst[:, :]))
        p.emit("vector", lambda e: e.tensor_scalar(
            m1n[:, :], m1[:, :], cst[:, C_NEG1:C_NEG1 + 1], None, ALU.mult))
        for s in range(NSUB):
            p.emit("vector", lambda e, s=s: e.tensor_scalar(
                lmask[:, s * NE:(s + 1) * NE], lgt[:, s * NE:(s + 1) * NE],
                m1[:, s:s + 1], None, ALU.is_lt))
        m1n_rdy = p.emit("vector", lambda e: e.drain(), inc="sV")
        for s in range(NSUB):
            p.emit("vector", lambda e, s=s: e.select(
                lp[:, s * NE:(s + 1) * NE], lmask[:, s * NE:(s + 1) * NE],
                lgt[:, s * NE:(s + 1) * NE], negbig[:, s * NE:(s + 1) * NE]))
        for s in range(NSUB):
            p.emit("vector", lambda e, s=s: e.tensor_reduce(
                m2[:, s:s + 1], lp[:, s * NE:(s + 1) * NE], AX.X, ALU.max))
        p.emit("vector", lambda e: e.tensor_copy(dmy[:, :], cst[:, :]))
        for s in range(NSUB):
            p.emit("vector", lambda e, s=s: e.tensor_scalar(
                keep[:, s * NE:(s + 1) * NE], lgt[:, s * NE:(s + 1) * NE],
                m2[:, s:s + 1], None, ALU.is_ge))
        keep_rdy = p.emit("vector", lambda e: e.drain(), inc="sV")
        exps = []
        for s in range(NSUB):
            exps.append(p.emit("scalar", lambda e, s=s: e.activation(
                e8g[:, s * NE:(s + 1) * NE], lgt[:, s * NE:(s + 1) * NE],
                AF.Exp, bias=m1n[:, s:s + 1]),
                waits=[("sV", m1n_rdy)], inc="sS"))
        p.emit("vector", lambda e: e.tensor_tensor(
            ekg[:, :], e8g[:, :], keep[:, :], ALU.mult),
            waits=[("sS", exps[-1]), ("sV", keep_rdy)])
        for s in range(NSUB):
            p.emit("vector", lambda e, s=s: e.tensor_reduce(
                zz[:, s:s + 1], ekg[:, s * NE:(s + 1) * NE], AX.X, ALU.add))
        p.emit("vector", lambda e: e.tensor_copy(dmy[:, :], cst[:, :]))
        p.emit("vector", lambda e: e.tensor_scalar(
            zz2[:, :], zz[:, :], cst[:, C_TWO:C_TWO + 1], None, ALU.mult))
        p.emit("vector", lambda e: e.tensor_copy(dmy[:, :], cst[:, :]))
        p.emit("vector", lambda e: e.reciprocal(zinv[:, :], zz2[:, :]))
        p.emit("vector", lambda e: e.tensor_copy(dmy[:, :], cst[:, :]))
        g8_last = None
        for s in range(NSUB):
            g8_last = p.emit("vector", lambda e, s=s: e.tensor_scalar(
                g8[:, s * NE:(s + 1) * NE], ekg[:, s * NE:(s + 1) * NE],
                zinv[:, s:s + 1], None, ALU.mult), inc="sV")

        def emit_gt_gb():
            p.emit("vector", lambda e: e.tensor_copy(dmy[:, :], cst[:, :]))
            g8b_rdy = p.emit("vector", lambda e: e.tensor_copy(g8b[:, :], g8[:, :]), inc="sV")
            b = tr["pt"]
            tr["pt"] ^= 1
            waits = [("sV", g8b_rdy)]
            fw = free.pop(("PT", b), None)
            if fw:
                waits.append(fw)
            v = None
            for s in range(NSUB):
                v = p.emit("tensor", lambda e, s=s, b=b: e.transpose(
                    PT[b][0:NE, 128 * s:128 * (s + 1)], g8b[:, s * NE:(s + 1) * NE],
                    ident[:, :]),
                    waits=waits if s == 0 else (),
                    inc="sT" if s == NSUB - 1 else None)
            gT_rdy = p.emit("scalar", lambda e, b=b: e.copy(gT[0:NE, :], PT[b][0:NE, 0:TT]),
                            waits=[("sT", v)], inc="sS")
            free[("PT", b)] = ("sS", gT_rdy)
            init_last = None
            for k in range(NSUB * 2):
                si, h = k // 2, k % 2
                bank = k % 2
                waits = [("sS", gT_rdy)] if k == 0 else []
                fw = bank_free_wait(bank)
                if fw:
                    waits.append(fw)
                v2 = p.emit("tensor", lambda e, si=si, h=h, bank=bank: e.matmul(
                    PM[bank][:, 0:512], gT[0:NE, 128 * si:128 * (si + 1)],
                    bB[0:NE, 512 * h:512 * (h + 1)], start=True, stop=True),
                    waits=waits, inc="sT")
                col = si * D + 512 * h
                iwaits = [("sT", v2)]
                if k == 0:
                    iwaits.extend(res_out_waits)
                init_last = p.emit("vector", lambda e, col=col, bank=bank: e.scalar_tensor_tensor(
                    res_out[:, col:col + 512], PM[bank][:, 0:512],
                    cst[:, C_NEG1:C_NEG1 + 1], res_in[:, col:col + 512],
                    ALU.bypass, ALU.add),
                    waits=iwaits, inc="sV")
                free[("PM", bank)] = ("sV", init_last)
            return init_last

        sub_done = {}
        ev = None
        pending = []
        flat = [(e_i, k) for e_i in range(NE) for k in range(NSUB * 2)]
        wq = wbuf = widx = None
        for idx, (e_i, k) in enumerate(flat):
            if k == 0:
                widx = wload_seq.index((tile, m, e_i))
                emit_wloads(widx + 1)
                wq = qW[(tile, m, e_i)]
                wbuf = widx % NWE
            si, h = k // 2, k % 2
            g2 = e_i * 8 + k
            bank = 2 + (g2 % 3)
            waits = list(xt_subs[si])
            if k == 0:
                waits.append(wq)
            fw = bank_free_wait(bank)
            if fw:
                waits.append(fw)
            last_grp = None
            for f in range(8):
                last_grp = p.emit("tensor", lambda e, si=si, h=h, f=f, bank=bank, wbuf=wbuf: e.matmul(
                    PM[bank][:, 0:512],
                    XT[:, f * TT + 128 * si: f * TT + 128 * (si + 1)],
                    We[wbuf][:, f * D + 512 * h: f * D + 512 * (h + 1)],
                    start=(f == 0), stop=(f == 7)),
                    waits=waits if f == 0 else (),
                    inc="sT" if f == 7 else None)
            col = si * D + 512 * h
            gcol = si * NE + e_i
            if idx < 2:
                pending.append((bank, col, gcol, last_grp))
            else:
                if idx == 2:
                    init_last = emit_gt_gb()
                    for (pbank, pcol, pgcol, pgrp) in pending:
                        ev = p.emit("vector", lambda e, col=pcol, bank=pbank, gcol=pgcol: e.scalar_tensor_tensor(
                            res_out[:, col:col + 512], PM[bank][:, 0:512],
                            g8[:, gcol:gcol + 1], res_out[:, col:col + 512],
                            ALU.mult, ALU.add),
                            waits=[("sT", pgrp), ("sV", init_last)], inc="sV")
                        free[("PM", pbank)] = ("sV", ev)
                    pending = []
                ev = p.emit("vector", lambda e, col=col, bank=bank, gcol=gcol: e.scalar_tensor_tensor(
                    res_out[:, col:col + 512], PM[bank][:, 0:512],
                    g8[:, gcol:gcol + 1], res_out[:, col:col + 512],
                    ALU.mult, ALU.add),
                    waits=[("sT", last_grp)], inc="sV")
                free[("PM", bank)] = ("sV", ev)
            if mid_hooks and idx in mid_hooks:
                mid_hooks[idx]()
            if k == NSUB * 2 - 1:
                wfree.setdefault(widx, [])
                wfree[widx].append(("sT", last_grp))
            if e_i == NE - 1 and h == 1:
                sub_done[si] = ev
        done = p.emit("vector", lambda e: e.drain(), waits=[("sV", ev)], inc="sV")
        return done, sub_done

    def attn_emit(tile, xb, ob_other, r1_done, sc32_wait, r1_subs=None):
        sw = None if r1_subs is None else [("sV", r1_subs[s]) for s in range(NSUB)]
        ew = [("sV", r1_done)] if r1_subs is None else []
        ssq_rdy = ln_stats_emit(R1, XH, extra_waits=ew, sub_waits=sw)
        rstd_rdy, _ = ln_var_emit(ssq_rdy)
        var, rstd, mu = st_["var"], st_["rstd"], st_["mu"]
        musq = st_["musq"]
        tmpv, lnt, r2c = st_["tmpv"], st_["lnt"], st_["r2c"]
        zz2 = st_["zz2"]
        p.emit("vector", lambda e: e.tensor_tensor(musq[:, :], var[:, :], rstd[:, :], ALU.mult),
               waits=[("sS", rstd_rdy)])
        p.emit("vector", lambda e: e.tensor_copy(dmy[:, :], cst[:, :]))
        tv = p.emit("vector", lambda e: e.tensor_tensor(tmpv[:, :], musq[:, :], rstd[:, :], ALU.mult), inc="sV")
        p.emit("scalar", lambda e: e.activation(lnt[:, :], tmpv[:, :], AF.Ln,
                                                bias=np.float32(1e-5)),
               waits=[("sV", tv)])
        p.emit("scalar", lambda e: e.activation(
            r2c[:, :], lnt[:, :], AF.Exp, scale=np.float32(-0.5)))
        iv_rdy = p.emit("scalar", lambda e: e.activation(
            st_["zinv"][:, :], lnt[:, :], AF.Exp, scale=np.float32(0.5)), inc="sS")
        p.emit("vector", lambda e: e.tensor_tensor(zz2[:, :], rstd[:, :], r2c[:, :], ALU.mult),
               waits=[("sS", iv_rdy)])
        p.emit("vector", lambda e: e.tensor_copy(dmy[:, :], cst[:, :]))
        nm2_rdy = p.emit("vector", lambda e: e.scalar_tensor_tensor(
            st_["nm"][:, :], mu[:, :], cst[:, C_NEG1:C_NEG1 + 1], zz2[:, :],
            ALU.mult, ALU.mult), inc="sV")
        subs = []
        for s in range(NSUB):
            subs.append(p.emit("scalar", lambda e, s=s: e.activation(
                XH[:, s * D:(s + 1) * D], R1[:, s * D:(s + 1) * D],
                AF.Identity, scale=zz2[:, s:s + 1], bias=st_["nm"][:, s:s + 1]),
                waits=[("sV", nm2_rdy)], inc="sS"))
        inv2 = st_["zinv"]
        for s in range(NSUB):
            p.emit("vector", lambda e, s=s: e.scalar_tensor_tensor(
                R1[:, s * D:(s + 1) * D], XH[:, s * D:(s + 1) * D],
                inv2[:, s:s + 1], R1[:, s * D:(s + 1) * D],
                ALU.mult, ALU.add),
                waits=[("sS", subs[s])], inc="sV")
        xt_rdy = None
        for s in range(NSUB):
            xt_rdy = transpose_sub(XH, s, ("sS", subs[s]), affine=True)
        if dbg == "B0" and tile == 0:
            fold_done = p.emit("vector", lambda e: e.drain(), inc="sV")
            dma("sync", d_r1, R1[:, :], "sO0", waits=[("sV", fold_done)])
            cv1 = p.emit("vector", lambda e: e.tensor_copy(dbg_f32[:, 0:NSUB * D], XH[:, :]),
                         waits=[("sV", a_sub[-1])], inc="sV")
            dma("sync", d_xh, dbg_f32[:, 0:NSUB * D], "sO0", waits=[("sV", cv1)])
            cv2 = p.emit("vector", lambda e: e.tensor_copy(dbg_f32[:, :], XT[:, :]),
                         waits=[("sO0", p.cnt["sO0"]), ("sV", cv1)], inc="sV")
            dma("sync", d_xt, dbg_f32[:, :], "sO0", waits=[("sV", cv2)])
            dma("sync", d_sc32, SC32[:, :], "sO0", waits=[("sV", cv2)])
            dma("sync", d_sc32b, SC32B[:, :], "sO0", waits=[("sV", cv2)])
            return None

        relu_cnt, sq_cnt, stage2_done = {}, {}, {}
        out_evac = {}

        def emit_stage2(h, g, k):
            if g in stage2_done:
                return
            jj, ic = k // 8, k % 8
            v = None
            for s in range(NSUB):
                waits = []
                if s == 0:
                    waits.append(("sV", sq_cnt[g]))
                if k == 0:
                    fw = bank_free_wait(s)
                    if fw:
                        waits.append(fw)
                v = p.emit("tensor", lambda e, s=s, jj=jj, ic=ic, g=g, k=k: e.matmul(
                    PM[s][:, 0:256],
                    Pp[g % 3][:, 128 * s:128 * (s + 1)],
                    wprojT[:, ic * D + 256 * jj: ic * D + 256 * (jj + 1)],
                    start=(k == 0), stop=(k == 31)),
                    waits=waits, inc="sT" if s == NSUB - 1 else None)
            stage2_done[g] = v

        g_global = 0
        nh_run = 1 if dbg in ("B3", "B4", "B5") else NH
        nj_run = 1 if dbg in ("B4", "B5") else 4
        ni_run = 1 if dbg == "B4" else 8
        for h in range(nh_run):
            for jj in range(nj_run):
                for ic in range(ni_run):
                    g = g_global
                    g_global += 1
                    k = jj * 8 + ic
                    sbank = 4 + (g % 2)
                    waits = []
                    if g == 0:
                        waits.extend(xt_rdy)
                    if g >= 2:
                        waits.append(("sS", relu_cnt[g - 2]))
                    if g < 2:
                        fw = bank_free_wait(sbank)
                        if fw:
                            waits.append(fw)
                    v = None
                    for dc in range(2):
                        cc_w = 2 * jj + dc
                        cc_a = 2 * h + dc
                        v = p.emit("tensor", lambda e, cc_w=cc_w, cc_a=cc_a, ic=ic, sbank=sbank, dc=dc: e.matmul(
                            PM[sbank][:, 0:512],
                            wfcT[:, cc_w * D + 128 * ic: cc_w * D + 128 * (ic + 1)],
                            XT[:, cc_a * TT:(cc_a + 1) * TT],
                            start=(dc == 0), stop=(dc == 1)),
                            waits=waits if dc == 0 else (),
                            inc="sT" if dc == 1 else None)
                    rwaits = [("sT", v)]
                    if g >= 2:
                        rwaits.append(("sV", sq_cnt[g - 2]))
                    relu_cnt[g] = p.emit("scalar", lambda e, g=g, sbank=sbank: e.activation(
                        RL[g % 2][:, :], PM[sbank][:, 0:512], AF.Relu),
                        waits=rwaits, inc="sS")
                    qwaits = [("sS", relu_cnt[g])]
                    if g >= 3:
                        qwaits.append(("sT", stage2_done[g - 3]))
                    sq_cnt[g] = p.emit("vector", lambda e, g=g: e.tensor_tensor(
                        Pp[g % 3][:, :], RL[g % 2][:, :], RL[g % 2][:, :], ALU.mult),
                        waits=qwaits, inc="sV")
                    if dbg == "B2" and tile == 0 and g == 0:
                        cv = p.emit("vector", lambda e: e.tensor_copy(
                            dbg_f32[:, 0:512], PM[4][:, 0:512]),
                            waits=[("sS", relu_cnt[0])], inc="sV")
                        cv2 = p.emit("vector", lambda e: e.tensor_copy(
                            dbg_f32[:, 512:1024], RL[0][:, :]), inc="sV")
                        cv3 = p.emit("vector", lambda e: e.tensor_copy(
                            dbg_f32[:, 1024:1536], Pp[0][:, :]),
                            waits=[("sV", sq_cnt[0])], inc="sV")
                        cv4 = p.emit("vector", lambda e: e.tensor_copy(
                            dbg_f32[:, 1536:2048], wfcT[:, 0:512]), inc="sV")
                        cv5 = p.emit("vector", lambda e: e.tensor_copy(
                            dbg_f32[:, 2048:2560], XT[:, 0:512]), inc="sV")
                        cv6 = p.emit("vector", lambda e: e.tensor_copy(
                            dbg_f32[:, 2560:3072], XT[:, 512:1024]),
                            waits=[("sV", cv3)], inc="sV")
                        dma("sync", d_xt, dbg_f32[:, :], "sO0", waits=[("sV", cv6)])
                        return None
                    if k >= 2:
                        emit_stage2(h, g - 2, k - 2)
            if dbg == "B4":
                emit_stage2(h, 0, 0)
            elif dbg == "B5":
                emit_stage2(h, g_global - 2, 6)
                emit_stage2(h, g_global - 1, 7)
            else:
                emit_stage2(h, g_global - 2, 30)
                emit_stage2(h, g_global - 1, 31)
            for s in range(NSUB):
                evv = p.emit("vector", lambda e, s=s, h=h: e.scalar_tensor_tensor(
                    R1[:, s * D + 256 * h: s * D + 256 * (h + 1)],
                    PM[s][:, 0:256], cst[:, C_NEG1:C_NEG1 + 1],
                    R1[:, s * D + 256 * h: s * D + 256 * (h + 1)],
                    ALU.bypass, ALU.add),
                    waits=[("sT", stage2_done[g_global - 1])], inc="sV")
                out_evac[(h, s)] = evv
                free[("PM", s)] = ("sV", evv)
        if g_global >= 2:
            free[("PM", 4)] = ("sS", relu_cnt[g_global - 2])
            free[("PM", 5)] = ("sS", relu_cnt[g_global - 1])
        x3_subs = {s: out_evac[(nh_run - 1, s)] for s in range(NSUB)}
        return p.emit("vector", lambda e: e.drain(),
                      waits=[("sV", out_evac[(nh_run - 1, NSUB - 1)])], inc="sV"), x3_subs

    x_scratch_done = {}
    out_free = {}
    subs_next = None
    for i in range(n_tiles):
        xb = i % 2
        ob = 0
        if i + 1 < n_tiles:
            waits = []
            if (i - 1) in x_scratch_done:
                waits.append(x_scratch_done[i - 1])
            sem = f"sQx{(i + 1) % 2}"
            qX.append((sem, dma("sync", X[(i + 1) % 2][:, :],
                                x_h[TT * (i + 1):TT * (i + 2), :].rearrange("(s q) f -> q s f", q=128),
                                sem, waits=waits)))
        if subs_next is None:
            subs = ln_emit(X[xb], XH1, extra_waits=[qX[i]], trash=XH1)
        else:
            subs = subs_next
        xt_subs = [transpose_sub(XH1, s, subs[s]) for s in range(NSUB)]
        r1_done, r1_subs = moe_emit(i, 0, xt_subs, R1, X[xb])
        if dbg == "A" and i == 0:
            cv1 = p.emit("vector", lambda e: e.tensor_copy(dbg_f32[:, 0:NSUB * D], XH[:, :]),
                         waits=[("sV", r1_done)], inc="sV")
            dma("sync", d_xh, dbg_f32[:, 0:NSUB * D], "sO0", waits=[("sV", cv1)])
            cv2 = p.emit("vector", lambda e: e.tensor_copy(dbg_f32[:, :], XT[:, :]),
                         waits=[("sO0", p.cnt["sO0"]), ("sV", cv1)], inc="sV")
            dma("sync", d_xt, dbg_f32[:, :], "sO0", waits=[("sV", cv2)])
            dma("sync", d_lgt, lgt[:, :], "sO0", waits=[("sV", r1_done)])
            dma("sync", d_g8, g8[:, :], "sO0", waits=[("sV", r1_done)])
            dma("sync", d_r1, R1[:, :], "sO0", waits=[("sV", r1_done)])
            break
        x3_res = attn_emit(i, xb, 1 - ob, r1_done, out_free.get(1 - ob),
                           r1_subs=r1_subs)
        if x3_res is None:
            break
        x3_done, x3_subs = x3_res
        if dbg in ("B3", "B4", "B5") and i == 0:
            dma("sync", d_r1, R1[:, :], "sO0", waits=[("sV", x3_done)])
            if dbg in ("B4", "B5"):
                cvp = p.emit("vector", lambda e: e.tensor_copy(
                    dbg_f32[:, 0:512], Pp[0][:, :]), waits=[("sV", x3_done)], inc="sV")
                cvr = p.emit("vector", lambda e: e.tensor_copy(
                    dbg_f32[:, 512:1024], RL[0][:, :]), inc="sV")
                cvs = p.emit("vector", lambda e: e.tensor_copy(
                    dbg_f32[:, 1024:1536], PM[4][:, 0:512]), inc="sV")
                cvs = p.emit("vector", lambda e: e.tensor_copy(
                    dbg_f32[:, 1536:2048], PM[5][:, 0:512]), inc="sV")
                dma("sync", d_xt, dbg_f32[:, :], "sO0", waits=[("sV", cvs)])
            break
        if dbg == "C0" and i == 0:
            dma("sync", d_r1, R1[:, :], "sO0", waits=[("sV", x3_done)])
        if dbg == "B0" and i == 0:
            break
        x_scratch_done[i] = ("sV", x3_done)
        if dbg == "B" and i == 0:
            dma("sync", d_r1, R1[:, :], "sO0", waits=[("sV", x3_done)])
            break
        if dbg == "C4" and i == 0:
            for _ in range(8):
                p.emit("vector", lambda e: e.tensor_copy(dmy[:, :], cst[:, :]))
                p.emit("scalar", lambda e: e.activation(dmy[:, :], cst[:, :], AF.Identity))
            dd = p.emit("vector", lambda e: e.drain(), waits=[("sV", x3_done)], inc="sV")
            dma("sync", d_r1, R1[:, :], "sO0", waits=[("sV", dd)])
            break
        subs3 = ln_emit(R1, XH, sub_waits=[("sV", x3_subs[s]) for s in range(NSUB)])
        if dbg == "C1" and i == 0:
            dma("sync", d_r1, R1[:, :], "sO0", waits=[("sS", subs3[-1])])
            break
        xt3 = [transpose_sub(XH, s, subs3[s]) for s in range(NSUB)]
        if dbg == "C2" and i == 0:
            dma("sync", d_r1, R1[:, :], "sO0", waits=list(xt3[-1]))
            break
        ow = [out_free[ob]] if ob in out_free else []
        holder = {}

        def ln1_stats(i=i, holder=holder):
            if i + 1 < n_tiles and dbg is None:
                holder["st"] = ln_stats_emit(X[(i + 1) % 2], XH1,
                                             extra_waits=[qX[i + 1]])

        def ln1_rest(i=i, holder=holder):
            if i + 1 < n_tiles and dbg is None:
                holder["subs"] = ln_chain_emit(X[(i + 1) % 2], XH1, holder["st"])

        out_done, _ = moe_emit(i, 1, xt3, OUT[ob], R1, res_out_waits=ow,
                               mid_hooks={3: ln1_stats, 13: ln1_rest})
        subs_next = holder.get("subs")
        if dbg == "C" and i == 0:
            cv1 = p.emit("vector", lambda e: e.tensor_copy(dbg_f32[:, 0:NSUB * D], XH[:, :]),
                         waits=[("sV", out_done)], inc="sV")
            dma("sync", d_xh, dbg_f32[:, 0:NSUB * D], "sO0", waits=[("sV", cv1)])
            cv2 = p.emit("vector", lambda e: e.tensor_copy(dbg_f32[:, :], XT[:, :]),
                         waits=[("sO0", p.cnt["sO0"]), ("sV", cv1)], inc="sV")
            dma("sync", d_xt, dbg_f32[:, :], "sO0", waits=[("sV", cv2)])
            dma("sync", d_lgt, lgt[:, :], "sO0", waits=[("sV", out_done)])
            dma("sync", d_g8, g8[:, :], "sO0", waits=[("sV", out_done)])
            dma("sync", d_r1, R1[:, :], "sO0", waits=[("sV", out_done)])
            dma("sync", d_sc32, OUT[ob][:, :], "sO0", waits=[("sV", out_done)])
            break
        dst = out_h[TT * i:TT * (i + 1), :].rearrange("(s q) f -> q s f", q=128)
        sem = f"sO{ob}"
        sq_ = dma("sync", dst, OUT[ob][:, :].rearrange("q (s f) -> q s f", f=D),
                  sem, waits=[("sV", out_done)])
        out_free[ob] = (sem, sq_)

    final_stores = [("sO0", p.cnt["sO0"]), ("sO1", p.cnt["sO1"])]

    sems = {s: es.enter_context(nc.semaphore(s))
            for s in ["sT", "sV", "sS", "sQc", "sQx0", "sQx1",
                      "sGs", "sGw0", "sGw1", "sGw2", "sO0", "sO1", "sB0", "sB1"]}
    block = es.enter_context(nc.Block())

    def mk(name):
        def f(eng):
            if name in ("tensor", "vector", "scalar"):
                eng.wait_ge(sems["sQc"], q_consts)
                eng.wait_ge(sems["sGs"], g_static)
            p.replay(name, eng, sems)
            if name == "sync":
                for s_, v_ in final_stores:
                    if v_:
                        eng.wait_ge(sems[s_], v_)
        return f

    block.sync(mk("sync"))
    block.gpsimd(mk("gpsimd"))
    block.vector(mk("vector"))
    block.scalar(mk("scalar"))
    block.tensor(mk("tensor"))

    es.close()
    return nc


def make_common(w_fc, w_proj, ln_g, ln_b, gate_w, gate_b, expert_w, expert_b):
    return {
        "wfcT": np.ascontiguousarray(np.asarray(w_fc, dtype=np.float32).T),
        "wprojT": np.ascontiguousarray(np.asarray(w_proj, dtype=np.float32).T),
        "gwT": np.ascontiguousarray(np.asarray(gate_w, dtype=np.float32).T),
        "ew": np.ascontiguousarray(np.asarray(expert_w, dtype=np.float32)),
        "eb": np.ascontiguousarray(np.asarray(expert_b, dtype=np.float32)),
        "gfm": np.ascontiguousarray(np.asarray(ln_g, dtype=np.float32).reshape(8, 128).T),
        "bvfm": np.ascontiguousarray(np.asarray(ln_b, dtype=np.float32).reshape(8, 128).T),
        "gb4": np.ascontiguousarray(np.tile(
            np.asarray(gate_b, dtype=np.float32), (128, NSUB))),
        "cst": np.ascontiguousarray(np.tile(np.array(
            [1.0 / D, -1.0, 2.0, 0, 0, 0, 0, 0], dtype=np.float32), (128, 1))),
        "ident": np.eye(128, dtype=ml_dtypes.bfloat16),
        "ident32": np.eye(128, dtype=np.float32),
    }


def kernel(L0_attn_residual, L0_attn_w_fc, L0_attn_w_proj, L0_attn_ln_g,
           L0_attn_ln_b, L0_ffn_gate_w, L0_ffn_gate_b, L0_ffn_expert_w,
           L0_ffn_expert_b):
    xfull = np.ascontiguousarray(
        np.asarray(L0_attn_residual, dtype=np.float32).reshape(-1, D))
    per = xfull.shape[0] // NCORE
    n_tiles = per // TT
    common = make_common(L0_attn_w_fc, L0_attn_w_proj, L0_attn_ln_g,
                         L0_attn_ln_b, L0_ffn_gate_w, L0_ffn_gate_b,
                         L0_ffn_expert_w, L0_ffn_expert_b)
    nc = build_program(n_tiles)
    in_maps = [dict(common, x=np.ascontiguousarray(xfull[c * per:(c + 1) * per]))
               for c in range(NCORE)]
    res = run_bass_kernel_spmd(nc, in_maps, core_ids=list(range(NCORE)))
    out = np.concatenate([res.results[c]["out"] for c in range(NCORE)], axis=0)
    return out.reshape(np.asarray(L0_attn_residual).shape).astype(np.float32)
